# version 22
# baseline (speedup 1.0000x reference)
import numpy as np
import ml_dtypes
import concourse.bass as bass
import concourse.mybir as mybir
from concourse.bass_utils import run_bass_kernel_spmd

F32 = mybir.dt.float32
BF16 = mybir.dt.bfloat16
I32 = mybir.dt.int32
AF = mybir.ActivationFunctionType
ALU = mybir.AluOpType
AX = mybir.AxisListType.X

L = 2048
DM = 1024
DI = 2048
DS = 8
DTR = 64
DFF = 4096
NB_DM = DM // 128
NB_DI = DI // 128
NB_FF = DFF // 128
NT = L // 128
EPS_LN = 1e-5
EPS_Q = 1e-5


class Sem:

    def __init__(self, nc, name):
        self._cm = nc.semaphore(name)
        self.h = self._cm.__enter__()
        self.n = 0

    def inc(self, instr, k=1):
        instr.then_inc(self.h, k)
        self.n += k
        return self.n


class Ring:

    def __init__(self, nc, name, slots=2):
        self.sems = [Sem(nc, f"{name}{i}") for i in range(slots)]
        self.slots = slots

    def start(self):
        return Stream(self)

    def barrier(self, eng):
        for s in self.sems:
            eng.wait_ge(s.h, s.n)


class Stream:
    def __init__(self, ring):
        self.ring = ring
        self.base = [s.n for s in ring.sems]

    def issue(self, instr, idx):
        self.ring.sems[idx % self.ring.slots].inc(instr, 16)

    def wait(self, eng, idx):
        p = idx % self.ring.slots
        eng.wait_ge(self.ring.sems[p].h, self.base[p] + 16 * (idx // self.ring.slots + 1))

    def wait_all(self, eng):
        for s in self.ring.sems:
            eng.wait_ge(s.h, s.n)

    def barrier_base(self, eng):
        for s, b in zip(self.ring.sems, self.base):
            eng.wait_ge(s.h, b)


def build_core(dbg=False):
    nc = bass.Bass("TRN2", target_bir_lowering=False)

    d_x = nc.dram_tensor("x", [L, DM], F32, kind="ExternalInput").ap()
    d_win = nc.dram_tensor("w_in", [DM, 2 * DI], F32, kind="ExternalInput").ap()
    d_pw = nc.dram_tensor("pw", [DI, DI], F32, kind="ExternalInput").ap()
    d_wx = nc.dram_tensor("w_x", [128, NB_DI * 80], F32, kind="ExternalInput").ap()
    d_wdt = nc.dram_tensor("w_dt", [DTR, DI], F32, kind="ExternalInput").ap()
    d_wout = nc.dram_tensor("w_out", [DI, DM], F32, kind="ExternalInput").ap()
    d_w1t = nc.dram_tensor("w1t", [DM, DFF], F32, kind="ExternalInput").ap()
    d_w2t = nc.dram_tensor("w2t", [DFF, DM], F32, kind="ExternalInput").ap()
    d_dwt = nc.dram_tensor("dwt", [128, NB_DI * 4], F32, kind="ExternalInput").ap()
    d_alog = nc.dram_tensor("a_log", [128, NB_DI * DS], F32, kind="ExternalInput").ap()
    d_dvec = nc.dram_tensor("d_vec", [128, NB_DI], F32, kind="ExternalInput").ap()
    d_cb = nc.dram_tensor("conv_bias", [128, NB_DI], F32, kind="ExternalInput").ap()
    d_bdt = nc.dram_tensor("b_dt", [128, NB_DI], F32, kind="ExternalInput").ap()
    d_b1 = nc.dram_tensor("b1", [128, NB_FF], F32, kind="ExternalInput").ap()
    d_b2 = nc.dram_tensor("b2", [1, DM], F32, kind="ExternalInput").ap()
    d_eye16 = nc.dram_tensor("eye16", [128, 128], BF16, kind="ExternalInput").ap()
    d_eyem = nc.dram_tensor("eyem", [128, 128], F32, kind="ExternalInput").ap()
    d_ind = nc.dram_tensor("ind", [16, 16 * 128], BF16, kind="ExternalInput").ap()
    d_ones_r = nc.dram_tensor("ones_r", [1, 128], F32, kind="ExternalInput").ap()
    d_ones_c = nc.dram_tensor("ones_c", [128, 1], F32, kind="ExternalInput").ap()

    d_out = nc.dram_tensor("out", [L, DM], F32, kind="ExternalOutput").ap()

    s_z = nc.dram_tensor("s_z", [NB_DI, 128, L], BF16, kind="Internal").ap()
    s_xc = nc.dram_tensor("s_xc", [NB_DI, 128, L], BF16, kind="Internal").ap()
    s_dl = nc.dram_tensor("s_dl", [NB_DI, 128, L], BF16, kind="Internal").ap()
    s_y2 = nc.dram_tensor("s_y2", [NB_DI, NT, 128, 128], BF16, kind="Internal").ap()
    s_h1 = nc.dram_tensor("s_h1", [NB_FF, NT, 128, 128], BF16, kind="Internal").ap()
    s_wq1 = nc.dram_tensor("s_wq1", [NB_DM, 128, DFF], BF16, kind="Internal").ap()
    s_wq2 = nc.dram_tensor("s_wq2", [NB_FF, 128, DM], BF16, kind="Internal").ap()
    s_wo = nc.dram_tensor("s_wo", [DI, DM], BF16, kind="Internal").ap()

    live = {}
    order = []

    def sb(name, shape, dt):
        cm = nc.sbuf_tensor(name, shape, dt)
        t = cm.__enter__()
        live[name] = cm
        order.append(name)
        return t

    def ps(name, shape, dt):
        cm = nc.psum_tensor(name, shape, dt)
        t = cm.__enter__()
        live[name] = cm
        return t

    def free(*names):
        for name in sorted(names, key=order.index, reverse=True):
            live.pop(name).__exit__(None, None, None)

    cpool = sb("cpool", [128, 8], F32)
    eye16 = sb("t_eye16", [128, 128], BF16)
    eyem = sb("t_eyem", [128, 128], F32)
    ind16 = sb("t_ind16", [16, 16 * 128], BF16)
    ones_r = sb("t_onesr", [1, 128], F32)
    ones_c = sb("t_onesc", [128, 1], F32)
    aneg = sb("aneg", [128, NB_DI * DS], F32)
    dtab = sb("dtab", [128, NB_DI], F32)
    cbias = sb("cbias", [128, NB_DI], F32)
    bdt = sb("bdt", [128, NB_DI], F32)
    b1t = sb("b1t", [128, NB_FF], F32)
    b2row = sb("b2row", [1, DM], F32)
    gtmp = sb("gtmp", [128, 40], F32)
    gcol = sb("gcol", [128, 6], F32)
    t_dwt = sb("t_dwt", [128, NB_DI * 4], F32)
    wx16 = sb("wx16", [128, NB_DI * 80], BF16)
    wdt16 = sb("wdt16", [64, DI], BF16)
    xnt = sb("xnt", [128, NB_DM * L], BF16)
    arena = sb("arena", [128, 79952], BF16)
    pb = [ps(f"pb{i}", [128, 512], F32) for i in range(8)]
    pt_t = [pb[6][:, 0:64].bitcast(BF16), pb[7][:, 0:64].bitcast(BF16)]

    ar32_all = arena[:].bitcast(F32)
    ar_i32_all = arena[:].bitcast(I32)

    def a16(off_b, cols):
        assert off_b % 2 == 0
        return arena[:, off_b // 2:off_b // 2 + cols]

    def a32(off_b, cols):
        assert off_b % 4 == 0
        return ar32_all[:, off_b // 4:off_b // 4 + cols]

    def ai32(off_b, cols):
        assert off_b % 4 == 0
        return ar_i32_all[:, off_b // 4:off_b // 4 + cols]

    sld = Sem(nc, "sld")
    lld = Sem(nc, "lld")
    spe = Sem(nc, "spe")
    sdv = Sem(nc, "sdv")
    sac = Sem(nc, "sac")
    sfr = Sem(nc, "sfr")
    sgp = Sem(nc, "sgp")
    lcast = Sem(nc, "lcast")
    RA = Ring(nc, "rA")
    RB = Ring(nc, "rB")
    RC = Ring(nc, "rC")
    RD = Ring(nc, "rD")
    RE = Ring(nc, "rE", slots=1)
    RF = Ring(nc, "rF", slots=1)
    RG = Ring(nc, "rG")
    RH = Ring(nc, "rH", slots=1)
    RL = Ring(nc, "rL", slots=6)
    RW = Ring(nc, "rW", slots=12)
    RS = Ring(nc, "rS", slots=2)
    RS3 = Ring(nc, "rS3", slots=3)
    RT = Ring(nc, "rT", slots=2)

    blk = nc.Block().__enter__()

    @blk.sync
    def _(s):
        for dst, src in [
            (eye16[:], d_eye16), (eyem[:], d_eyem), (ind16[:], d_ind),
            (ones_r[:], d_ones_r), (ones_c[:], d_ones_c), (b2row[:], d_b2),
            (aneg[:], d_alog), (dtab[:], d_dvec), (cbias[:], d_cb),
            (bdt[:], d_bdt), (b1t[:], d_b1), (t_dwt[:], d_dwt),
        ]:
            sld.inc(s.dma_start(dst, src), 16)
    p0_sld = sld.n

    @blk.gpsimd
    def _(g):
        lld.inc(g.dma_start(wx16[:], d_wx), 16)
        lld.inc(g.dma_start(wdt16[:], d_wdt), 16)

    @blk.vector
    def _(v):
        consts = [0.0, 1.0, -1.0, EPS_LN, -0.5, 1.0 / DM, 0.5, 0.0]
        for i, c in enumerate(consts):
            ci = v.memset(cpool[:, i:i + 1], c)
        sdv.inc(ci)

    nc.const_aps.register(F32, -1.0, cpool[:, 2:3])
    nc.const_aps.register(F32, EPS_LN, cpool[:, 3:4])
    nc.const_aps.register(F32, -0.5, cpool[:, 4:5])
    nc.const_aps.register(F32, 1.0 / DM, cpool[:, 5:6])
    nc.const_aps.register(F32, 0.5, cpool[:, 6:7])

    @blk.scalar
    def _(sc):
        sc.wait_ge(sld.h, p0_sld)
        sac.inc(sc.activation(aneg[:], aneg[:], AF.Exp))

    @blk.vector
    def _(v):
        v.wait_ge(sac.h, sac.n)
        sdv.inc(v.tensor_scalar(aneg[:], aneg[:], -1.0, None, ALU.mult))

    xtok = a32(0, NT * DM)
    lnsq = a32(65536, 2 * DM)
    lnst = a32(73728, 128)
    lnxn = a16(74240, 2 * DM)
    qp1 = [a32(111104 + s * 16384, 4096) for s in range(2)]
    qtr = a32(143872, 2048)

    @blk.sync
    def _(s):
        sld.inc(s.dma_start(xtok[:].rearrange("p (t d) -> p t d", t=NT),
                    d_x.rearrange("(t p) d -> p t d", p=128)), 16)
    p1_x = sld.n

    rs1 = RS.start()

    def q1load(sc, slot, c):
        if c < 8:
            return sc.dma_start(qp1[slot][:], d_w1t[c * 128:(c + 1) * 128, :])
        k = c - 8
        return sc.dma_start(
            qp1[slot][:].rearrange("p (a d) -> p a d", a=4),
            d_w2t[k * 512:(k + 1) * 512, :].rearrange("(a p) d -> p a d",
                                                      p=128))

    @blk.scalar
    def _(sc):
        rs1.issue(q1load(sc, 0, 0), 0)
        rs1.issue(q1load(sc, 1, 1), 1)

    ac0 = sac.n
    dv0 = sdv.n

    @blk.scalar
    def _(sc):
        sc.wait_ge(sld.h, p1_x)
        sc.wait_ge(sdv.h, dv0)
        for t in range(NT):
            st = lnst[:, t * 8:t * 8 + 8]
            sac.inc(sc.activation(lnsq[:, (t % 2) * DM:(t % 2 + 1) * DM],
                                  xtok[:, t * DM:(t + 1) * DM], AF.Square,
                                  accum_out=st[:, 1:2]))

    @blk.vector
    def _(v):
        v.wait_ge(sac.h, ac0 + NT)
        for t in range(NT):
            v.tensor_reduce(lnst[:, t * 8:t * 8 + 1], xtok[:, t * DM:(t + 1) * DM],
                            AX, ALU.add)
        for t in range(NT):
            v.tensor_scalar(lnst[:, t * 8 + 2:t * 8 + 3], lnst[:, t * 8:t * 8 + 1],
                            1.0 / DM, None, ALU.mult)
        for t in range(NT):
            v.tensor_scalar(lnst[:, t * 8 + 4:t * 8 + 5], lnst[:, t * 8:t * 8 + 1],
                            -1.0 / DM, None, ALU.mult)
        for t in range(NT):
            v.tensor_scalar(lnst[:, t * 8 + 3:t * 8 + 4], lnst[:, t * 8 + 1:t * 8 + 2],
                            1.0 / DM, None, ALU.mult)
        for t in range(NT):
            v.tensor_tensor(lnst[:, t * 8 + 5:t * 8 + 6], lnst[:, t * 8 + 2:t * 8 + 3],
                            lnst[:, t * 8 + 4:t * 8 + 5], ALU.mult)
        for t in range(NT):
            sdv.inc(v.tensor_tensor(lnst[:, t * 8 + 5:t * 8 + 6],
                                    lnst[:, t * 8 + 3:t * 8 + 4],
                                    lnst[:, t * 8 + 5:t * 8 + 6], ALU.add))

    dv_var = sdv.n
    ac1 = sac.n

    @blk.scalar
    def _(sc):
        sc.wait_ge(sdv.h, dv_var)
        for t in range(NT):
            sc.activation(lnst[:, t * 8 + 6:t * 8 + 7], lnst[:, t * 8 + 5:t * 8 + 6],
                          AF.Ln, bias=EPS_LN)
        for t in range(NT):
            sac.inc(sc.activation(lnst[:, t * 8 + 7:t * 8 + 8],
                                  lnst[:, t * 8 + 6:t * 8 + 7], AF.Exp, scale=-0.5))

    dv1 = sdv.n

    ac_q1 = sac.n

    @blk.scalar
    def _(sc):
        for c in range(16):
            rs1.wait(sc, c)
            sc.activation(qtr[:], qp1[c % 2][:, 0:2048], AF.Abs,
                          accum_out=gtmp[:, 2 * c:2 * c + 1])
            sac.inc(sc.activation(qtr[:], qp1[c % 2][:, 2048:4096], AF.Abs,
                                  accum_out=gtmp[:, 2 * c + 1:2 * c + 2]))
            if c + 2 < 16:
                sc.wait_ge(sac.h, ac_q1 + c + 1)
                rs1.issue(q1load(sc, c % 2, c + 2), c + 2)
    pass1_ac_end = sac.n

    @blk.vector
    def _(v):
        v.wait_ge(sac.h, ac1 + NT)
        for t in range(NT):
            sdv.inc(v.tensor_tensor(lnst[:, t * 8 + 5:t * 8 + 6],
                                    lnst[:, t * 8 + 4:t * 8 + 5],
                                    lnst[:, t * 8 + 7:t * 8 + 8], ALU.mult))

    pe0 = spe.n
    apply_done = [None] * NT
    evac_done = [None] * (NT * NB_DM)

    @blk.vector
    def _(v):
        def evacs(tp):
            for d in range(NB_DM):
                i = tp * NB_DM + d
                v.wait_ge(spe.h, pe0 + i + 1)
                evac_done[i] = sdv.inc(
                    v.tensor_copy(xnt[:, d * L + tp * 128:d * L + (tp + 1) * 128],
                                  pt_t[i % 2][:, 0:128]))
        for t in range(NT):
            st = lnst[:, t * 8:t * 8 + 8]
            if t >= 2:
                v.wait_ge(spe.h, pe0 + (t - 1) * NB_DM)
            apply_done[t] = sdv.inc(
                v.tensor_scalar(lnxn[:, (t % 2) * DM:(t % 2 + 1) * DM],
                                xtok[:, t * DM:(t + 1) * DM],
                                st[:, 7:8], st[:, 5:6], ALU.mult, ALU.add))
            if t >= 1:
                evacs(t - 1)
        evacs(NT - 1)

    @blk.tensor
    def _(t_):
        for t in range(NT):
            t_.wait_ge(sdv.h, apply_done[t])
            for d in range(NB_DM):
                i = t * NB_DM + d
                if i >= 2:
                    t_.wait_ge(sdv.h, evac_done[i - 2])
                spe.inc(t_.transpose(
                    pt_t[i % 2][:, 0:128],
                    lnxn[:, (t % 2) * DM + d * 128:(t % 2) * DM + (d + 1) * 128],
                    eye16[:]))

    p1_dv_end = sdv.n

    xin = a16(0, NB_DI * (L + 3))
    wbA = a16(78336, 16384)
    stgZ = a16(111104, 2 * L)

    @blk.vector
    def _(v):
        sdv.inc(v.memset(xin[:], 0.0))
    p2_dv_memset = sdv.n

    lld_h0 = lld.n

    @blk.gpsimd
    def _(g):
        for k in range(NB_DM):
            lld.inc(g.dma_start(wbA[:, k * 2048:(k + 1) * 2048],
                                d_win[k * 128:(k + 1) * 128, 0:2048]), 16)
    p2_w0 = lld.n

    fr0 = sfr.n
    pe_p2 = spe.n
    zsp = RB.start()
    ac_p2 = sac.n

    @blk.tensor
    def _(t):
        t.wait_ge(lld.h, p2_w0)
        t.wait_ge(sdv.h, p1_dv_end)
        t.wait_ge(sdv.h, p2_dv_memset)
        for f in range(NB_DI):
            grp = (f % 2) * 4
            if f >= 2:
                t.wait_ge(sfr.h, fr0 + f - 1)
            for q in range(4):
                for k in range(NB_DM):
                    mm = t.matmul(pb[grp + q][:],
                             wbA[:, k * 2048 + f * 128:k * 2048 + f * 128 + 128],
                             xnt[:, k * L + q * 512:k * L + (q + 1) * 512],
                             start=(k == 0), stop=(k == NB_DM - 1))
            spe.inc(mm)
    pe_p2_h0 = spe.n

    @blk.vector
    def _(v):
        for f in range(NB_DI):
            v.wait_ge(spe.h, pe_p2 + f + 1)
            grp = (f % 2) * 4
            for q in range(4):
                ci = v.tensor_copy(
                    xin[:, f * (L + 3) + 3 + q * 512:f * (L + 3) + 3 + (q + 1) * 512],
                    pb[grp + q][:])
            sfr.inc(ci)

    dv_g0 = sdv.n
    dv_gred, dv_gsc, dv_gcol = dv_g0 + 1, dv_g0 + 2, dv_g0 + 3
    pe_g0 = spe.n
    pe_gsum, pe_gbc = pe_g0 + 1, pe_g0 + 2
    NQ1 = 8 * 128 * 4096

    @blk.vector
    def _(v):
        v.wait_ge(sac.h, pass1_ac_end)
        v.tensor_reduce(gtmp[:, 36:37], gtmp[:, 0:16], AX, ALU.add)
        sdv.inc(v.tensor_reduce(gtmp[:, 37:38], gtmp[:, 16:32], AX, ALU.add))
        v.wait_ge(spe.h, pe_gsum)
        v.tensor_scalar(gtmp[0:1, 38:39], pb[0][0:1, 0:1],
                        0.5 / NQ1, 0.5 * EPS_Q, ALU.mult, ALU.add)
        sdv.inc(v.tensor_scalar(gtmp[0:1, 39:40], pb[1][0:1, 0:1],
                                0.5 / NQ1, 0.5 * EPS_Q, ALU.mult, ALU.add))
        v.wait_ge(spe.h, pe_gbc)
        v.tensor_copy(gcol[:, 1:2], pb[0][:, 0:1])
        v.tensor_scalar(gcol[:, 2:3], pb[0][:, 0:1], -1.0, None, ALU.mult)
        v.tensor_copy(gcol[:, 3:4], pb[1][:, 0:1])
        sdv.inc(v.tensor_scalar(gcol[:, 4:5], pb[1][:, 0:1], -1.0, None,
                                ALU.mult))

    @blk.tensor
    def _(t):
        t.wait_ge(sdv.h, dv_gred)
        t.wait_ge(sfr.h, fr0 + 15)
        t.matmul(pb[0][0:1, 0:1], ones_c[:], gtmp[:, 36:37], start=True, stop=True)
        mm = t.matmul(pb[1][0:1, 0:1], ones_c[:], gtmp[:, 37:38], start=True, stop=True)
        spe.inc(mm)
        t.wait_ge(sdv.h, dv_gsc)
        t.matmul(pb[0][:, 0:1], ones_r[:], gtmp[0:1, 38:39], start=True, stop=True)
        mm = t.matmul(pb[1][:, 0:1], ones_r[:], gtmp[0:1, 39:40], start=True, stop=True)
        spe.inc(mm)

    @blk.gpsimd
    def _(g):
        g.wait_ge(spe.h, pe_p2_h0)
        for k in range(NB_DM):
            lld.inc(g.dma_start(wbA[:, k * 2048:(k + 1) * 2048],
                                d_win[k * 128:(k + 1) * 128, 2048:4096]), 16)
    p2_w1 = lld.n

    @blk.tensor
    def _(t):
        t.wait_ge(lld.h, p2_w1)
        t.wait_ge(sdv.h, dv_gcol)
        for i in range(NB_DI):
            f = NB_DI + i
            grp = (f % 2) * 4
            if f <= 17:
                t.wait_ge(sfr.h, fr0 + f - 1)
            else:
                t.wait_ge(sac.h, ac_p2 + f - 17)
            for q in range(4):
                for k in range(NB_DM):
                    mm = t.matmul(pb[grp + q][:],
                             wbA[:, k * 2048 + i * 128:k * 2048 + i * 128 + 128],
                             xnt[:, k * L + q * 512:k * L + (q + 1) * 512],
                             start=(k == 0), stop=(k == NB_DM - 1))
            spe.inc(mm)
    pe_p2_h1 = spe.n

    @blk.scalar
    def _(sc):
        for i in range(NB_DI):
            sc.wait_ge(spe.h, pe_p2_h0 + 2 + i + 1)
            if i >= 2:
                zsp.wait(sc, i - 2)
            grp = ((NB_DI + i) % 2) * 4
            for q in range(4):
                ai = sc.activation(stgZ[:, (i % 2) * L + q * 512:(i % 2) * L + (q + 1) * 512],
                                   pb[grp + q][:], AF.Silu)
            sac.inc(ai)

    @blk.sync
    def _(s):
        for i in range(NB_DI):
            s.wait_ge(sac.h, ac_p2 + i + 1)
            zsp.issue(s.dma_start(s_z[i], stgZ[:, (i % 2) * L:(i % 2) * L + L]), i)

    p2_fr_end = sfr.n

    conv64 = a16(65632, NB_DI * L)
    cvd = a16(131232, 2 * 512)

    dv_cvd = sdv.n
    pe_p3 = spe.n
    fr_p3 = sfr.n

    @blk.vector
    def _(v):
        v.wait_ge(spe.h, pe_p2_h1)
        v.wait_ge(sac.h, pass1_ac_end)
        for j in range(NB_DI):
            if j >= 2:
                v.wait_ge(spe.h, pe_p3 + j - 1)
                jj = j - 2
                if jj == 11:
                    zsp.wait(v, 14)
                    zsp.wait(v, 15)
                grp = (jj % 2) * 4
                for q in range(4):
                    ci = v.tensor_copy(conv64[:, jj * L + q * 512:jj * L + (q + 1) * 512],
                                       pb[grp + q][:])
                sfr.inc(ci)
            for k in range(4):
                ci = v.tensor_scalar(cvd[:, (j % 2) * 512 + k * 128:(j % 2) * 512 + (k + 1) * 128],
                                     eyem[:], t_dwt[:, j * 4 + k:j * 4 + k + 1], None, ALU.mult)
            sdv.inc(ci)
        for jj in (NB_DI - 2, NB_DI - 1):
            v.wait_ge(spe.h, pe_p3 + jj + 1)
            grp = (jj % 2) * 4
            for q in range(4):
                ci = v.tensor_copy(conv64[:, jj * L + q * 512:jj * L + (q + 1) * 512],
                                   pb[grp + q][:])
            sfr.inc(ci)

    @blk.tensor
    def _(t):
        t.wait_ge(sfr.h, p2_fr_end)
        t.wait_ge(sac.h, ac_p2 + NB_DI)
        for j in range(NB_DI):
            grp = (j % 2) * 4
            t.wait_ge(sdv.h, dv_cvd + j + 1)
            if j >= 2:
                t.wait_ge(sfr.h, fr_p3 + j - 1)
            for q in range(4):
                for k in range(4):
                    mm = t.matmul(pb[grp + q][:],
                             cvd[:, (j % 2) * 512 + k * 128:(j % 2) * 512 + (k + 1) * 128],
                             xin[:, j * (L + 3) + k + q * 512:j * (L + 3) + k + q * 512 + 512],
                             start=(k == 0), stop=(k == 3))
            spe.inc(mm)

    p3_fr_end = sfr.n
    p3_dv_end = sdv.n

    pw64 = a16(0, NB_DI * DI)
    xcs = a16(131232, 2 * L)

    @blk.gpsimd
    def _(g):
        g.wait_ge(spe.h, pe_p2_h1)
        for k in range(NB_DI):
            g.wait_ge(spe.h, pe_p3 + k + 1)
            lld.inc(g.dma_start(pw64[:, k * 2048:(k + 1) * 2048],
                                d_pw[k * 128:(k + 1) * 128, :]), 16)
    p4_w = lld.n

    pe_p4 = spe.n
    ac_p4 = sac.n
    xsp = RB.start()

    @blk.tensor
    def _(t):
        t.wait_ge(lld.h, p4_w)
        t.wait_ge(sfr.h, p3_fr_end)
        for f in range(NB_DI):
            grp = (f % 2) * 4
            if f >= 2:
                t.wait_ge(sac.h, ac_p4 + f - 1)
            for q in range(4):
                for k in range(NB_DI):
                    mm = t.matmul(pb[grp + q][:],
                             pw64[:, k * 2048 + f * 128:k * 2048 + f * 128 + 128],
                             conv64[:, k * L + q * 512:k * L + (q + 1) * 512],
                             start=(k == 0), stop=(k == NB_DI - 1))
            spe.inc(mm)

    @blk.scalar
    def _(sc):
        for f in range(NB_DI):
            sc.wait_ge(spe.h, pe_p4 + f + 1)
            if f >= 2:
                xsp.wait(sc, f - 2)
            for q in range(4):
                ai = sc.activation(xcs[:, (f % 2) * L + q * 512:(f % 2) * L + (q + 1) * 512],
                                   pb[(f % 2) * 4 + q][:], AF.Silu, bias=cbias[:, f:f + 1])
            sac.inc(ai)

    @blk.sync
    def _(s):
        for f in range(NB_DI):
            s.wait_ge(sac.h, ac_p4 + f + 1)
            xsp.issue(s.dma_start(s_xc[f], xcs[:, (f % 2) * L:(f % 2) * L + L]), f)

    ac_p4_end = sac.n

    qin2 = [a32(139424 + s * 4096, 1024) for s in range(3)]
    qout2 = [a16(151712 + s * 2048, 1024) for s in range(2)]
    NQC = 64

    dv_q2b = sdv.n
    rs2 = RS3.start()
    spq = RT.start()

    def q1done(c):
        return dv_q2b + c + 1

    def q2src(c):
        if c < 32:
            return d_w1t[(c // 4) * 128:(c // 4 + 1) * 128,
                         (c % 4) * 1024:(c % 4 + 1) * 1024]
        return d_w2t[(c - 32) * 128:(c - 31) * 128, :]

    def q2dst(c):
        if c < 32:
            return s_wq1[c // 4][:, (c % 4) * 1024:(c % 4 + 1) * 1024]
        return s_wq2[c - 32]

    @blk.vector
    def _(v):
        for c in range(NQC):
            thr, nthr = (1, 2) if c < 32 else (3, 4)
            rs2.wait(v, c)
            if c >= 2:
                spq.wait(v, c - 2)
            v.tensor_scalar(qout2[c % 2][:], qin2[c % 3][:],
                            gcol[:, nthr:nthr + 1], -1.0, ALU.is_gt, ALU.add)
            sdv.inc(v.scalar_tensor_tensor(qout2[c % 2][:], qin2[c % 3][:],
                                           gcol[:, thr:thr + 1], qout2[c % 2][:],
                                           ALU.is_ge, ALU.add))

    @blk.gpsimd
    def _(g):
        def spill(c):
            spq.issue(g.dma_start(q2dst(c), qout2[c % 2][:]), c)

        g.wait_ge(sac.h, pass1_ac_end)
        for c in range(NQC):
            if c >= 3:
                g.wait_ge(sdv.h, q1done(c - 3))
                spill(c - 3)
            rs2.issue(g.dma_start(qin2[c % 3][:], q2src(c)), c)
        for c in (NQC - 3, NQC - 2, NQC - 1):
            g.wait_ge(sdv.h, q1done(c))
            spill(c)
        for k in range(NB_DI):
            lcast.inc(g.dma_start(s_wo[k * 128:(k + 1) * 128, :],
                                  d_wout[k * 128:(k + 1) * 128, :]), 16)

    xcr = a16(106496, 2 * L)
    d016 = a16(8192, L)[0:64, :]
    bc16 = a16(12288, L)[0:16, :]
    p5t = a32(16384, L)
    stgD = a16(24576, 2 * L)

    pe_p5 = spe.n
    xcl = RA.start()

    @blk.sync
    def _(s):
        for k in range(NB_DI):
            xsp.wait(s, k)
            if k >= 2:
                s.wait_ge(spe.h, pe_p5 + k - 1)
            xcl.issue(s.dma_start(xcr[:, (k % 2) * L:(k % 2) * L + L], s_xc[k]), k)

    @blk.tensor
    def _(t):
        t.wait_ge(sac.h, ac_p4 + 15)
        for k in range(NB_DI):
            xcl.wait(t, k)
            for q in range(4):
                mm = t.matmul(pb[q][0:80, :],
                         wx16[:, k * 80:(k + 1) * 80],
                         xcr[:, (k % 2) * L + q * 512:(k % 2) * L + (q + 1) * 512],
                         start=(k == 0), stop=(k == NB_DI - 1))
            spe.inc(mm)
    pe_p5a = spe.n

    @blk.vector
    def _(v):
        v.wait_ge(spe.h, pe_p5a)
        for q in range(4):
            v.tensor_copy(d016[:, q * 512:(q + 1) * 512], pb[q][0:64, :])
            ci = v.tensor_copy(bc16[:, q * 512:(q + 1) * 512], pb[q][64:80, :])
        sdv.inc(ci)

    dv_p5bc = sdv.n

    brep = a16(40960, DS * L)
    crep = a16(73728, DS * L)

    pe_p5c = spe.n
    dv_rep0 = sdv.n

    @blk.tensor
    def _(t):
        t.wait_ge(sdv.h, dv_p5bc)
        t.wait_ge(sac.h, ac_p4_end)
        for n in range(2 * DS):
            grp = (n % 2) * 4
            if n >= 2:
                t.wait_ge(sdv.h, dv_rep0 + n - 1)
            for q in range(4):
                mm = t.matmul(pb[grp + q][:], ind16[:, n * 128:(n + 1) * 128],
                         bc16[:, q * 512:(q + 1) * 512], start=True, stop=True)
            spe.inc(mm)
    pe_p5c_end = spe.n

    @blk.vector
    def _(v):
        for n in range(2 * DS):
            v.wait_ge(spe.h, pe_p5c + n + 1)
            grp = (n % 2) * 4
            dst = brep if n < DS else crep
            nn = n % DS
            for q in range(4):
                ci = v.tensor_copy(dst[:, nn * L + q * 512:nn * L + (q + 1) * 512],
                                   pb[grp + q][:])
            sdv.inc(ci)

    dv_p5_end = sdv.n

    pe_p5b = spe.n
    ac_p5b = sac.n
    dsp = RB.start()

    @blk.tensor
    def _(t):
        t.wait_ge(sdv.h, dv_rep0 + 16)
        for f in range(NB_DI):
            grp = (f % 2) * 4
            if f >= 2:
                t.wait_ge(sac.h, ac_p5b + 2 * (f - 1))
            for q in range(4):
                mm = t.matmul(pb[grp + q][:],
                         wdt16[:, f * 128:(f + 1) * 128],
                         d016[:, q * 512:(q + 1) * 512],
                         start=True, stop=True)
            spe.inc(mm)
    pe_p5b_end = spe.n

    @blk.scalar
    def _(sc):
        RB.barrier(sc)
        for f in range(NB_DI):
            sc.wait_ge(spe.h, pe_p5b + f + 1)
            if f >= 2:
                dsp.wait(sc, f - 2)
            grp = (f % 2) * 4
            for q in range(4):
                ai = sc.activation(p5t[:, q * 512:(q + 1) * 512], pb[grp + q][:],
                                   AF.Exp, bias=bdt[:, f:f + 1])
            sac.inc(ai)
            for q in range(4):
                ai = sc.activation(stgD[:, (f % 2) * L + q * 512:(f % 2) * L + (q + 1) * 512],
                                   p5t[:, q * 512:(q + 1) * 512], AF.Ln, bias=1.0)
            sac.inc(ai)

    ac_p5_end = sac.n

    @blk.sync
    def _(s):
        for f in range(NB_DI):
            s.wait_ge(sac.h, ac_p5b + 2 * (f + 1))
            dsp.issue(s.dma_start(s_dl[f], stgD[:, (f % 2) * L:(f % 2) * L + L]), f)

    dl_s = [a16(0, L), a16(4096, L)]
    xc_s = [a16(8192, L), a16(12288, L)]
    zj_s = [a16(16384, L), a16(20480, L)]
    a_s = [a16(24576, L), a16(28672, L)]
    uj_s = a16(32768, L)
    b_s = a16(36864, L)
    h_s = a16(106496, L)
    xcd_s = a16(110592, L)
    ty2_s = a16(114688, L)
    hc_s = [a16(118784 + n * 4096, L) for n in range(DS)]
    wq1r = [a16(151552, 1024), a16(153600, 1024)]
    h1stg = a16(155808, L)

    ac_p6 = sac.n
    dv_p6 = sdv.n
    pe_p6 = spe.n
    dl6 = RA.start()
    xc6 = RC.start()
    z6 = RD.start()
    ysp = RE.start()
    rg = RG.start()
    h8 = RB.start()

    def dv_scan(i):
        j, n = divmod(i, DS)
        return dv_p6 + 31 * j + 3 * n + 3

    def dv_hc(i):
        j, n = divmod(i, DS)
        return dv_p6 + 31 * j + 3 * n + 4

    def dv_t1(j):
        return dv_p6 + 31 * j + 30

    def dv_y2(j):
        return dv_p6 + 31 * (j + 1)

    def pe_sumb(j):
        return pe_p6 + 3 * j + 1

    def ffn1_done(f):
        return pe_p6 + 3 * (f // 2) + 2 + (f % 2)

    sac_a = {}
    sac_relu = [None] * NB_FF

    @blk.scalar
    def _(sc):
        def emit_relu(f):
            sc.wait_ge(spe.h, ffn1_done(f))
            if f >= 1:
                h8.wait(sc, f - 1)
            for q in range(4):
                ai = sc.activation(h1stg[:, q * 512:(q + 1) * 512],
                                   pb[4 + q][:], AF.Relu, bias=b1t[:, f:f + 1])
            sac_relu[f] = sac.inc(ai)

        h8.barrier_base(sc)
        dsp.wait(sc, 14)
        dsp.wait(sc, 15)
        for j in range(NB_DI):
            dl6.wait(sc, j)
            for n in range(DS):
                i = j * DS + n
                if i >= 2:
                    sc.wait_ge(sdv.h, dv_scan(i - 2))
                sac_a[i] = sac.inc(sc.activation(
                    a_s[i % 2], dl_s[j % 2], AF.Exp,
                    scale=aneg[:, j * DS + n:j * DS + n + 1]))
                if n == 1 and j >= 1:
                    emit_relu(2 * j - 2)
                if n == 4 and j >= 1:
                    emit_relu(2 * j - 1)
        emit_relu(2 * NB_DI - 2)
        emit_relu(2 * NB_DI - 1)

    @blk.sync
    def _(s):
        def loads(j):
            dsp.wait(s, j)
            dl6.issue(s.dma_start(dl_s[j % 2], s_dl[j]), j)
            xc6.issue(s.dma_start(xc_s[j % 2], s_xc[j]), j)
            z6.issue(s.dma_start(zj_s[j % 2], s_z[j]), j)

        s.wait_ge(spe.h, pe_p5c_end)
        spq.wait_all(s)
        loads(0)
        loads(1)
        for j in range(NB_DI):
            if j >= 1:
                s.wait_ge(sdv.h, dv_y2(j - 1))
                ysp.issue(s.dma_start(s_y2[j - 1].rearrange("t p c -> p t c"),
                                      ty2_s.rearrange("p (t c) -> p t c", t=NT)), j - 1)
                if j + 1 < NB_DI:
                    loads(j + 1)
            for f in (2 * j - 2, 2 * j - 1):
                if 0 <= f < NB_FF:
                    s.wait_ge(sac.h, sac_relu[f])
                    h8.issue(s.dma_start(
                        s_h1[f].rearrange("t p c -> p t c"),
                        h1stg[:].rearrange("p (t c) -> p t c", t=NT)), f)
            for f in (2 * j, 2 * j + 1):
                if f >= 2:
                    s.wait_ge(spe.h, ffn1_done(f - 2))
                rg.issue(s.dma_start(
                    wq1r[f % 2].rearrange("p (k c) -> p k c", k=NB_DM),
                    s_wq1[:, :, f * 128:(f + 1) * 128].rearrange("k p c -> p k c")), f)
        for f in (NB_FF - 2, NB_FF - 1):
            s.wait_ge(sac.h, sac_relu[f])
            h8.issue(s.dma_start(
                s_h1[f].rearrange("t p c -> p t c"),
                h1stg[:].rearrange("p (t c) -> p t c", t=NT)), f)
        s.wait_ge(sdv.h, dv_y2(NB_DI - 1))
        ysp.issue(s.dma_start(s_y2[NB_DI - 1].rearrange("t p c -> p t c"),
                              ty2_s.rearrange("p (t c) -> p t c", t=NT)), NB_DI - 1)

    @blk.vector
    def _(v):
        h8.barrier_base(v)
        spq.wait_all(v)
        for j in range(NB_DI):
            dl6.wait(v, j)
            xc6.wait(v, j)
            if j >= 1:
                v.wait_ge(spe.h, pe_sumb(j - 1))
            sdv.inc(v.tensor_tensor(uj_s, dl_s[j % 2], xc_s[j % 2], ALU.mult))
            for n in range(DS):
                i = j * DS + n
                v.wait_ge(sac.h, sac_a[i])
                sdv.inc(v.tensor_tensor(b_s, uj_s, brep[:, n * L:(n + 1) * L], ALU.mult))
                sdv.inc(v.tensor_tensor_scan(h_s, a_s[i % 2], b_s, 0.0, ALU.mult, ALU.add))
                sdv.inc(v.tensor_tensor(hc_s[n], h_s, crep[:, n * L:(n + 1) * L], ALU.mult))
            xc6.wait(v, j)
            sdv.inc(v.tensor_scalar(xcd_s, xc_s[j % 2], dtab[:, j:j + 1], None, ALU.mult))
            v.wait_ge(spe.h, pe_sumb(j))
            if j >= 1:
                ysp.wait(v, j - 1)
            for q in range(4):
                sdv.inc(v.tensor_tensor(ty2_s[:, q * 512:(q + 1) * 512], pb[q][:],
                                        xcd_s[:, q * 512:(q + 1) * 512], ALU.add))
            z6.wait(v, j)
            sdv.inc(v.tensor_tensor(ty2_s, ty2_s, zj_s[j % 2], ALU.mult))
    assert sdv.n == dv_p6 + 31 * NB_DI

    @blk.tensor
    def _(t):
        for j in range(NB_DI):
            if j >= 1:
                t.wait_ge(sdv.h, dv_t1(j - 1))
            for n in range(DS):
                t.wait_ge(sdv.h, dv_hc(j * DS + n))
                for q in range(4):
                    mm = t.matmul(pb[q][:], eye16[:],
                                  hc_s[n][:, q * 512:(q + 1) * 512],
                                  start=(n == 0), stop=(n == DS - 1))
            spe.inc(mm)
            for f in (2 * j, 2 * j + 1):
                rg.wait(t, f)
                if f >= 1:
                    t.wait_ge(sac.h, sac_relu[f - 1])
                for q4 in range(4):
                    for k in range(NB_DM):
                        mm = t.matmul(pb[4 + q4][:],
                                      wq1r[f % 2][:, k * 128:(k + 1) * 128],
                                      xnt[:, k * L + q4 * 512:k * L + (q4 + 1) * 512],
                                      start=(k == 0), stop=(k == NB_DM - 1))
                spe.inc(mm)
    pe_p6_end = spe.n
    assert pe_p6_end == pe_p6 + 3 * NB_DI

    p6_dv_end = sdv.n

    wo16 = a16(0, NB_DI * DM)
    wq2f = a16(32768, NB_FF * DM)
    lr6 = [a16(98304 + s * 1024, 512) for s in range(6)]
    xre2 = [a32(110592 + s * 8192, 2 * DM) for s in range(2)]
    outst2 = [a32(126976 + s * 8192, 2 * DM) for s in range(2)]
    b2bc = a32(143360, DM)

    NKK = NB_DI + NB_FF
    pe_pt0 = spe.n
    dv_pt0 = sdv.n
    dv_b2bc = dv_pt0 + 1

    def pe_kk(i):
        return pe_pt0 + 2 + i

    def dv_evac(g8):
        return dv_pt0 + 2 + g8

    def lhs_consumed(c):
        g, qq = c // 24, (c % 24) // 2
        return pe_kk(g * NKK + 4 * qq + 3)

    wring = RW.start()
    lr = RL.start()
    xrl = RC.start()
    org = RD.start()

    @blk.sync
    def _(s):
        s.wait_ge(sdv.h, p6_dv_end)
        s.wait_ge(spe.h, pe_p6_end)
        s.wait_ge(lcast.h, 16 * NB_DI)
        h8.wait_all(s)
        ysp.wait_all(s)
        for g8 in range(8):
            if g8 >= 2:
                s.wait_ge(sdv.h, dv_evac(g8 - 2))
            xrl.issue(s.dma_start(
                xre2[g8 % 2].rearrange("p (t d) -> p t d", t=2),
                d_x.rearrange("(t p) d -> p t d", p=128)[:, 2 * g8:2 * g8 + 2, :]), g8)
            for q4 in range(NKK // 4):
                q = g8 * (NKK // 4) + q4
                if g8 == 0:
                    for kk in range(4 * q4, 4 * q4 + 4):
                        if kk >= 12:
                            s.wait_ge(spe.h, pe_kk(kk - 12))
                        if kk < NB_DI:
                            wring.issue(s.dma_start(
                                wo16[:, kk * 1024:(kk + 1) * 1024],
                                s_wo[kk * 128:(kk + 1) * 128, :]), kk)
                        else:
                            wring.issue(s.dma_start(
                                wq2f[:, (kk - NB_DI) * 1024:(kk - NB_DI + 1) * 1024],
                                s_wq2[kk - NB_DI]), kk)
                j0 = 4 * q4
                srcb = (s_y2[j0:j0 + 4] if j0 < NB_DI else s_h1[j0 - NB_DI:j0 - NB_DI + 4])
                for tt in range(2):
                    c = q * 2 + tt
                    if c >= 6:
                        s.wait_ge(spe.h, lhs_consumed(c - 6))
                    lr.issue(s.dma_start(
                        lr6[c % 6].rearrange("p (j c) -> p j c", j=4),
                        srcb[:, 2 * g8 + tt].rearrange("j p c -> p j c")), c)
            if g8 >= 1:
                s.wait_ge(sdv.h, dv_evac(g8 - 1))
                org.issue(s.dma_start(
                    d_out.rearrange("(t p) d -> p t d", p=128)[:, 2 * (g8 - 1):2 * g8, :],
                    outst2[(g8 - 1) % 2].rearrange("p (t d) -> p t d", t=2)), g8 - 1)
        s.wait_ge(sdv.h, dv_evac(7))
        org.issue(s.dma_start(
            d_out.rearrange("(t p) d -> p t d", p=128)[:, 14:16, :],
            outst2[1].rearrange("p (t d) -> p t d", t=2)), 7)
        org.wait_all(s)

    @blk.tensor
    def _(t):
        t.wait_ge(sld.h, p0_sld)
        t.wait_ge(sac.h, sac_relu[NB_FF - 1])
        mm = t.matmul(pb[4][:], ones_r[:], b2row[:, 0:512], start=True, stop=True)
        mm = t.matmul(pb[5][:], ones_r[:], b2row[:, 512:1024], start=True, stop=True)
        spe.inc(mm)
        t.wait_ge(sdv.h, p6_dv_end)
        for g8 in range(8):
            B = (g8 % 2) * 4
            if g8 == 1:
                t.wait_ge(sdv.h, dv_b2bc)
            if g8 >= 2:
                t.wait_ge(sdv.h, dv_evac(g8 - 2))
            for kk in range(NKK):
                q = (g8 * NKK + kk) // 4
                if g8 == 0:
                    wring.wait(t, kk)
                if kk % 4 == 0:
                    lr.wait(t, 2 * q)
                    lr.wait(t, 2 * q + 1)
                w = wo16 if kk < NB_DI else wq2f
                co = (kk if kk < NB_DI else kk - NB_DI) * 1024
                lo = (kk % 4) * 128
                for tt in range(2):
                    for h in range(2):
                        mm = t.matmul(pb[B + tt * 2 + h][:],
                                      lr6[(2 * q + tt) % 6][:, lo:lo + 128],
                                      w[:, co + h * 512:co + (h + 1) * 512],
                                      start=(kk == 0), stop=(kk == NKK - 1))
                spe.inc(mm)

    @blk.vector
    def _(v):
        v.wait_ge(spe.h, pe_pt0 + 1)
        v.tensor_copy(b2bc[:, 0:512], pb[4][:])
        ci = v.tensor_copy(b2bc[:, 512:1024], pb[5][:])
        sdv.inc(ci)
        for g8 in range(8):
            B = (g8 % 2) * 4
            v.wait_ge(spe.h, pe_kk(g8 * NKK + NKK - 1))
            xrl.wait(v, g8)
            if g8 >= 2:
                org.wait(v, g8 - 2)
            xs = xre2[g8 % 2]
            v.tensor_tensor(xs[:, 0:1024], xs[:, 0:1024], b2bc[:], ALU.add)
            v.tensor_tensor(xs[:, 1024:2048], xs[:, 1024:2048], b2bc[:], ALU.add)
            for tt in range(2):
                for h in range(2):
                    c0 = tt * 1024 + h * 512
                    ci = v.tensor_tensor(outst2[g8 % 2][:, c0:c0 + 512],
                                         pb[B + tt * 2 + h][:],
                                         xs[:, c0:c0 + 512], ALU.add)
            sdv.inc(ci)

    if dbg:
        d_dbg = {}
        for nm, st_t, shp in [("dbg_z", s_z, [NB_DI, 128, L]),
                              ("dbg_xc", s_xc, [NB_DI, 128, L]),
                              ("dbg_dl", s_dl, [NB_DI, 128, L]),
                              ("dbg_y2", s_y2, [NB_DI, NT, 128, 128]),
                              ("dbg_wq1", s_wq1, [NB_DM, 128, DFF]),
                              ("dbg_h1", s_h1, [NB_FF, NT, 128, 128])]:
            d_dbg[nm] = (nc.dram_tensor(nm, shp, BF16, kind="ExternalOutput").ap(), st_t)
        d_xnt_o = nc.dram_tensor("dbg_xnt", [128, NB_DM * L], BF16, kind="ExternalOutput").ap()
        d_gcol_o = nc.dram_tensor("dbg_gcol", [128, 6], F32, kind="ExternalOutput").ap()
        d_gtmp_o = nc.dram_tensor("dbg_gtmp", [128, 40], F32, kind="ExternalOutput").ap()

        @blk.sync
        def _(s):
            org.wait_all(s)
            for nm, (dst, st_t) in d_dbg.items():
                sld.inc(s.dma_start(dst, st_t), 16)
            sld.inc(s.dma_start(d_xnt_o, xnt[:]), 16)
            sld.inc(s.dma_start(d_gcol_o, gcol[:]), 16)
            sld.inc(s.dma_start(d_gtmp_o, gtmp[:]), 16)
            s.wait_ge(sld.h, sld.n)

    blk.__exit__(None, None, None)
    for cm in reversed(list(live.values())):
        cm.__exit__(None, None, None)
    nc.finalize()
    return nc


def kernel(**inputs):
    xfull = np.ascontiguousarray(inputs["L0_attn_x"], dtype=np.float32)
    w_in = np.ascontiguousarray(inputs["L0_attn_W_in"], dtype=np.float32)
    pw = np.ascontiguousarray(inputs["L0_attn_pw_kernel"], dtype=np.float32)
    w_x = np.ascontiguousarray(
        np.asarray(inputs["L0_attn_W_x"], dtype=np.float32)
        .reshape(NB_DI, 128, 80).transpose(1, 0, 2).reshape(128, NB_DI * 80))
    w_dt = np.ascontiguousarray(inputs["L0_attn_W_dt"], dtype=np.float32)
    w_out = np.ascontiguousarray(inputs["L0_attn_W_out"], dtype=np.float32)
    w1t = np.ascontiguousarray(np.asarray(inputs["L0_ffn_W1"], dtype=np.float32).T)
    w2t = np.ascontiguousarray(np.asarray(inputs["L0_ffn_W2"], dtype=np.float32).T)
    def blocky(vec, nb):
        return np.ascontiguousarray(
            np.asarray(vec, dtype=np.float32).reshape(nb, 128).T)

    dwt = np.ascontiguousarray(
        np.asarray(inputs["L0_attn_dw_kernel"], dtype=np.float32).T
        .reshape(NB_DI, 128, 4).transpose(1, 0, 2).reshape(128, NB_DI * 4))
    alog = np.ascontiguousarray(
        np.asarray(inputs["L0_attn_A_log"], dtype=np.float32)
        .reshape(NB_DI, 128, DS).transpose(1, 0, 2).reshape(128, NB_DI * DS))
    dvec = blocky(inputs["L0_attn_D"], NB_DI)
    cb = blocky(inputs["L0_attn_conv_bias"], NB_DI)
    bdtv = blocky(inputs["L0_attn_b_dt"], NB_DI)
    b1 = blocky(inputs["L0_ffn_b1"], NB_FF)
    b2 = np.ascontiguousarray(np.asarray(inputs["L0_ffn_b2"], dtype=np.float32).reshape(1, DM))

    eye16 = np.eye(128, dtype=ml_dtypes.bfloat16)
    eyem = np.eye(128, dtype=np.float32)
    ind = np.zeros((16, 16 * 128), dtype=ml_dtypes.bfloat16)
    for ssel in range(16):
        ind[ssel, ssel * 128:(ssel + 1) * 128] = 1.0
    ones_r = np.ones((1, 128), dtype=np.float32)
    ones_c = np.ones((128, 1), dtype=np.float32)

    nc = build_core()
    shared = dict(w_in=w_in, pw=pw, w_x=w_x, w_dt=w_dt, w_out=w_out, w1t=w1t,
                  w2t=w2t, dwt=dwt, a_log=alog, d_vec=dvec, conv_bias=cb,
                  b_dt=bdtv, b1=b1, b2=b2, eye16=eye16, eyem=eyem, ind=ind,
                  ones_r=ones_r, ones_c=ones_c)
    in_maps = [dict(shared, x=np.ascontiguousarray(xfull[b])) for b in range(8)]
    res = run_bass_kernel_spmd(nc, in_maps, core_ids=list(range(8)))
    return np.stack([res.results[b]["out"] for b in range(8)], axis=0)


# version 24 (final)
# speedup vs baseline: 1.0223x; 1.0223x over previous
import numpy as np
import ml_dtypes
import concourse.bass as bass
import concourse.mybir as mybir
from concourse.bass_utils import run_bass_kernel_spmd

F32 = mybir.dt.float32
BF16 = mybir.dt.bfloat16
I32 = mybir.dt.int32
AF = mybir.ActivationFunctionType
ALU = mybir.AluOpType
AX = mybir.AxisListType.X

L = 2048
DM = 1024
DI = 2048
DS = 8
DTR = 64
DFF = 4096
NB_DM = DM // 128
NB_DI = DI // 128
NB_FF = DFF // 128
NT = L // 128
EPS_LN = 1e-5
EPS_Q = 1e-5


class Sem:

    def __init__(self, nc, name):
        self._cm = nc.semaphore(name)
        self.h = self._cm.__enter__()
        self.n = 0

    def inc(self, instr, k=1):
        instr.then_inc(self.h, k)
        self.n += k
        return self.n


class Ring:

    def __init__(self, nc, name, slots=2):
        self.sems = [Sem(nc, f"{name}{i}") for i in range(slots)]
        self.slots = slots

    def start(self):
        return Stream(self)

    def barrier(self, eng):
        for s in self.sems:
            eng.wait_ge(s.h, s.n)


class Stream:
    def __init__(self, ring):
        self.ring = ring
        self.base = [s.n for s in ring.sems]

    def issue(self, instr, idx):
        self.ring.sems[idx % self.ring.slots].inc(instr, 16)

    def wait(self, eng, idx):
        p = idx % self.ring.slots
        eng.wait_ge(self.ring.sems[p].h, self.base[p] + 16 * (idx // self.ring.slots + 1))

    def wait_all(self, eng):
        for s in self.ring.sems:
            eng.wait_ge(s.h, s.n)

    def barrier_base(self, eng):
        for s, b in zip(self.ring.sems, self.base):
            eng.wait_ge(s.h, b)


def build_core(dbg=False):
    nc = bass.Bass("TRN2", target_bir_lowering=False)

    d_x = nc.dram_tensor("x", [L, DM], F32, kind="ExternalInput").ap()
    d_win = nc.dram_tensor("w_in", [DM, 2 * DI], F32, kind="ExternalInput").ap()
    d_pw = nc.dram_tensor("pw", [DI, DI], F32, kind="ExternalInput").ap()
    d_wx = nc.dram_tensor("w_x", [128, NB_DI * 80], F32, kind="ExternalInput").ap()
    d_wdt = nc.dram_tensor("w_dt", [DTR, DI], F32, kind="ExternalInput").ap()
    d_wout = nc.dram_tensor("w_out", [DI, DM], F32, kind="ExternalInput").ap()
    d_w1t = nc.dram_tensor("w1t", [DM, DFF], F32, kind="ExternalInput").ap()
    d_w2t = nc.dram_tensor("w2t", [DFF, DM], F32, kind="ExternalInput").ap()
    d_dwt = nc.dram_tensor("dwt", [128, NB_DI * 4], F32, kind="ExternalInput").ap()
    d_alog = nc.dram_tensor("a_log", [128, NB_DI * DS], F32, kind="ExternalInput").ap()
    d_dvec = nc.dram_tensor("d_vec", [128, NB_DI], F32, kind="ExternalInput").ap()
    d_cb = nc.dram_tensor("conv_bias", [128, NB_DI], F32, kind="ExternalInput").ap()
    d_bdt = nc.dram_tensor("b_dt", [128, NB_DI], F32, kind="ExternalInput").ap()
    d_b1 = nc.dram_tensor("b1", [128, NB_FF], F32, kind="ExternalInput").ap()
    d_b2 = nc.dram_tensor("b2", [1, DM], F32, kind="ExternalInput").ap()
    d_eye16 = nc.dram_tensor("eye16", [128, 128], BF16, kind="ExternalInput").ap()
    d_eyem = nc.dram_tensor("eyem", [128, 128], F32, kind="ExternalInput").ap()
    d_ind = nc.dram_tensor("ind", [16, 16 * 128], BF16, kind="ExternalInput").ap()
    d_ones_r = nc.dram_tensor("ones_r", [1, 128], F32, kind="ExternalInput").ap()
    d_ones_c = nc.dram_tensor("ones_c", [128, 1], F32, kind="ExternalInput").ap()

    d_out = nc.dram_tensor("out", [L, DM], F32, kind="ExternalOutput").ap()

    s_z = nc.dram_tensor("s_z", [NB_DI, 128, L], BF16, kind="Internal").ap()
    s_xc = nc.dram_tensor("s_xc", [NB_DI, 128, L], BF16, kind="Internal").ap()
    s_dl = nc.dram_tensor("s_dl", [NB_DI, 128, L], BF16, kind="Internal").ap()
    s_y2 = nc.dram_tensor("s_y2", [NB_DI, NT, 128, 128], BF16, kind="Internal").ap()
    s_h1 = nc.dram_tensor("s_h1", [NB_FF, NT, 128, 128], BF16, kind="Internal").ap()
    s_wq1 = nc.dram_tensor("s_wq1", [NB_DM, 128, DFF], BF16, kind="Internal").ap()
    s_wq2 = nc.dram_tensor("s_wq2", [NB_FF, 128, DM], BF16, kind="Internal").ap()
    s_wo = nc.dram_tensor("s_wo", [DI, DM], BF16, kind="Internal").ap()

    live = {}
    order = []

    def sb(name, shape, dt):
        cm = nc.sbuf_tensor(name, shape, dt)
        t = cm.__enter__()
        live[name] = cm
        order.append(name)
        return t

    def ps(name, shape, dt):
        cm = nc.psum_tensor(name, shape, dt)
        t = cm.__enter__()
        live[name] = cm
        return t

    def free(*names):
        for name in sorted(names, key=order.index, reverse=True):
            live.pop(name).__exit__(None, None, None)

    cpool = sb("cpool", [128, 8], F32)
    eye16 = sb("t_eye16", [128, 128], BF16)
    eyem = sb("t_eyem", [128, 128], F32)
    ind16 = sb("t_ind16", [16, 16 * 128], BF16)
    ones_r = sb("t_onesr", [1, 128], F32)
    ones_c = sb("t_onesc", [128, 1], F32)
    aneg = sb("aneg", [128, NB_DI * DS], F32)
    dtab = sb("dtab", [128, NB_DI], F32)
    cbias = sb("cbias", [128, NB_DI], F32)
    bdt = sb("bdt", [128, NB_DI], F32)
    b1t = sb("b1t", [128, NB_FF], F32)
    b2row = sb("b2row", [1, DM], F32)
    gtmp = sb("gtmp", [128, 40], F32)
    gcol = sb("gcol", [128, 6], F32)
    t_dwt = sb("t_dwt", [128, NB_DI * 4], F32)
    wx16 = sb("wx16", [128, NB_DI * 80], BF16)
    wdt16 = sb("wdt16", [64, DI], BF16)
    xnt = sb("xnt", [128, NB_DM * L], BF16)
    arena = sb("arena", [128, 79952], BF16)
    pb = [ps(f"pb{i}", [128, 512], F32) for i in range(8)]
    pt_t = [pb[6][:, 0:64].bitcast(BF16), pb[7][:, 0:64].bitcast(BF16)]

    ar32_all = arena[:].bitcast(F32)
    ar_i32_all = arena[:].bitcast(I32)

    def a16(off_b, cols):
        assert off_b % 2 == 0
        return arena[:, off_b // 2:off_b // 2 + cols]

    def a32(off_b, cols):
        assert off_b % 4 == 0
        return ar32_all[:, off_b // 4:off_b // 4 + cols]

    def ai32(off_b, cols):
        assert off_b % 4 == 0
        return ar_i32_all[:, off_b // 4:off_b // 4 + cols]

    sld = Sem(nc, "sld")
    lld = Sem(nc, "lld")
    spe = Sem(nc, "spe")
    sdv = Sem(nc, "sdv")
    sac = Sem(nc, "sac")
    sfr = Sem(nc, "sfr")
    sgp = Sem(nc, "sgp")
    lcast = Sem(nc, "lcast")
    RA = Ring(nc, "rA")
    RB = Ring(nc, "rB")
    RC = Ring(nc, "rC")
    RD = Ring(nc, "rD")
    RE = Ring(nc, "rE", slots=1)
    RF = Ring(nc, "rF", slots=1)
    RG = Ring(nc, "rG")
    RH = Ring(nc, "rH", slots=1)
    RL = Ring(nc, "rL", slots=6)
    RW = Ring(nc, "rW", slots=12)
    RS = Ring(nc, "rS", slots=2)
    RS3 = Ring(nc, "rS3", slots=3)
    RT = Ring(nc, "rT", slots=2)
    RX = Ring(nc, "rX", slots=4)

    blk = nc.Block().__enter__()

    @blk.sync
    def _(s):
        for dst, src in [
            (eye16[:], d_eye16), (eyem[:], d_eyem), (ind16[:], d_ind),
            (ones_r[:], d_ones_r), (ones_c[:], d_ones_c), (b2row[:], d_b2),
            (aneg[:], d_alog), (dtab[:], d_dvec), (cbias[:], d_cb),
            (bdt[:], d_bdt), (b1t[:], d_b1), (t_dwt[:], d_dwt),
        ]:
            sld.inc(s.dma_start(dst, src), 16)
    p0_sld = sld.n

    @blk.gpsimd
    def _(g):
        lld.inc(g.dma_start(wx16[:], d_wx), 16)
        lld.inc(g.dma_start(wdt16[:], d_wdt), 16)

    @blk.vector
    def _(v):
        consts = [0.0, 1.0, -1.0, EPS_LN, -0.5, 1.0 / DM, 0.5, 0.0]
        for i, c in enumerate(consts):
            ci = v.memset(cpool[:, i:i + 1], c)
        sdv.inc(ci)

    nc.const_aps.register(F32, -1.0, cpool[:, 2:3])
    nc.const_aps.register(F32, EPS_LN, cpool[:, 3:4])
    nc.const_aps.register(F32, -0.5, cpool[:, 4:5])
    nc.const_aps.register(F32, 1.0 / DM, cpool[:, 5:6])
    nc.const_aps.register(F32, 0.5, cpool[:, 6:7])

    @blk.scalar
    def _(sc):
        sc.wait_ge(sld.h, p0_sld)
        sac.inc(sc.activation(aneg[:], aneg[:], AF.Exp))

    @blk.vector
    def _(v):
        v.wait_ge(sac.h, sac.n)
        sdv.inc(v.tensor_scalar(aneg[:], aneg[:], -1.0, None, ALU.mult))

    xtok = a32(0, NT * DM)
    lnsq = a32(65536, 2 * DM)
    lnst = a32(73728, 128)
    lnxn = a16(74240, 2 * DM)
    qp1 = [a32(111104 + s * 16384, 4096) for s in range(2)]
    qtr = a32(143872, 2048)

    px = RX.start()

    @blk.sync
    def _(s):
        for g in range(4):
            px.issue(s.dma_start(
                xtok[:, g * 4 * DM:(g + 1) * 4 * DM].rearrange(
                    "p (t d) -> p t d", t=4),
                d_x.rearrange("(t p) d -> p t d", p=128)[:, 4 * g:4 * g + 4, :]),
                g)

    rs1 = RS.start()

    def q1load(sc, slot, c):
        if c < 8:
            return sc.dma_start(qp1[slot][:], d_w1t[c * 128:(c + 1) * 128, :])
        k = c - 8
        return sc.dma_start(
            qp1[slot][:].rearrange("p (a d) -> p a d", a=4),
            d_w2t[k * 512:(k + 1) * 512, :].rearrange("(a p) d -> p a d",
                                                      p=128))

    @blk.scalar
    def _(sc):
        rs1.issue(q1load(sc, 0, 0), 0)
        rs1.issue(q1load(sc, 1, 1), 1)

    ac0 = sac.n
    dv0 = sdv.n

    @blk.scalar
    def _(sc):
        sc.wait_ge(sdv.h, dv0)
        for t in range(NT):
            if t % 4 == 0:
                px.wait(sc, t // 4)
            st = lnst[:, t * 8:t * 8 + 8]
            sac.inc(sc.activation(lnsq[:, (t % 2) * DM:(t % 2 + 1) * DM],
                                  xtok[:, t * DM:(t + 1) * DM], AF.Square,
                                  accum_out=st[:, 1:2]))

    @blk.vector
    def _(v):
        v.wait_ge(sac.h, ac0 + NT)
        for t in range(NT):
            v.tensor_reduce(lnst[:, t * 8:t * 8 + 1], xtok[:, t * DM:(t + 1) * DM],
                            AX, ALU.add)
        for t in range(NT):
            v.tensor_scalar(lnst[:, t * 8 + 2:t * 8 + 3], lnst[:, t * 8:t * 8 + 1],
                            1.0 / DM, None, ALU.mult)
        for t in range(NT):
            v.tensor_scalar(lnst[:, t * 8 + 4:t * 8 + 5], lnst[:, t * 8:t * 8 + 1],
                            -1.0 / DM, None, ALU.mult)
        for t in range(NT):
            v.tensor_scalar(lnst[:, t * 8 + 3:t * 8 + 4], lnst[:, t * 8 + 1:t * 8 + 2],
                            1.0 / DM, None, ALU.mult)
        for t in range(NT):
            v.tensor_tensor(lnst[:, t * 8 + 5:t * 8 + 6], lnst[:, t * 8 + 2:t * 8 + 3],
                            lnst[:, t * 8 + 4:t * 8 + 5], ALU.mult)
        for t in range(NT):
            sdv.inc(v.tensor_tensor(lnst[:, t * 8 + 5:t * 8 + 6],
                                    lnst[:, t * 8 + 3:t * 8 + 4],
                                    lnst[:, t * 8 + 5:t * 8 + 6], ALU.add))

    dv_var = sdv.n
    ac1 = sac.n

    @blk.scalar
    def _(sc):
        sc.wait_ge(sdv.h, dv_var)
        for t in range(NT):
            sc.activation(lnst[:, t * 8 + 6:t * 8 + 7], lnst[:, t * 8 + 5:t * 8 + 6],
                          AF.Ln, bias=EPS_LN)
        for t in range(NT):
            sac.inc(sc.activation(lnst[:, t * 8 + 7:t * 8 + 8],
                                  lnst[:, t * 8 + 6:t * 8 + 7], AF.Exp, scale=-0.5))

    dv1 = sdv.n

    @blk.vector
    def _(v):
        v.wait_ge(sac.h, ac1 + NT)
        for t in range(NT):
            sdv.inc(v.tensor_tensor(lnst[:, t * 8 + 5:t * 8 + 6],
                                    lnst[:, t * 8 + 4:t * 8 + 5],
                                    lnst[:, t * 8 + 7:t * 8 + 8], ALU.mult))

    pe0 = spe.n
    apply_done = [None] * NT
    evac_done = [None] * (NT * NB_DM)

    @blk.vector
    def _(v):
        def evacs(tp):
            for d in range(0, NB_DM, 2):
                i = tp * NB_DM + d
                v.wait_ge(spe.h, pe0 + i + 1)
                evac_done[i] = ('dv', sdv.inc(
                    v.tensor_copy(xnt[:, d * L + tp * 128:d * L + (tp + 1) * 128],
                                  pt_t[i % 2][:, 0:128])))
        for t in range(NT):
            st = lnst[:, t * 8:t * 8 + 8]
            if t >= 2:
                v.wait_ge(spe.h, pe0 + (t - 1) * NB_DM)
            apply_done[t] = sdv.inc(
                v.tensor_scalar(lnxn[:, (t % 2) * DM:(t % 2 + 1) * DM],
                                xtok[:, t * DM:(t + 1) * DM],
                                st[:, 7:8], st[:, 5:6], ALU.mult, ALU.add))
            if t >= 1:
                evacs(t - 1)
        evacs(NT - 1)

    @blk.scalar
    def _(sc):
        for tp in range(NT):
            for d in range(1, NB_DM, 2):
                i = tp * NB_DM + d
                sc.wait_ge(spe.h, pe0 + i + 1)
                evac_done[i] = ('ac', sac.inc(
                    sc.activation(xnt[:, d * L + tp * 128:d * L + (tp + 1) * 128],
                                  pt_t[i % 2][:, 0:128], AF.Copy)))
    p1_ac_end = sac.n

    ac_q1 = sac.n

    @blk.scalar
    def _(sc):
        for c in range(16):
            rs1.wait(sc, c)
            sc.activation(qtr[:], qp1[c % 2][:, 0:2048], AF.Abs,
                          accum_out=gtmp[:, 2 * c:2 * c + 1])
            sac.inc(sc.activation(qtr[:], qp1[c % 2][:, 2048:4096], AF.Abs,
                                  accum_out=gtmp[:, 2 * c + 1:2 * c + 2]))
            if c + 2 < 16:
                sc.wait_ge(sac.h, ac_q1 + c + 1)
                rs1.issue(q1load(sc, c % 2, c + 2), c + 2)
    pass1_ac_end = sac.n

    @blk.tensor
    def _(t_):
        for t in range(NT):
            t_.wait_ge(sdv.h, apply_done[t])
            for d in range(NB_DM):
                i = t * NB_DM + d
                if i >= 2:
                    eng, cnt = evac_done[i - 2]
                    t_.wait_ge(sdv.h if eng == 'dv' else sac.h, cnt)
                spe.inc(t_.transpose(
                    pt_t[i % 2][:, 0:128],
                    lnxn[:, (t % 2) * DM + d * 128:(t % 2) * DM + (d + 1) * 128],
                    eye16[:]))

    p1_dv_end = sdv.n

    xin = a16(0, NB_DI * (L + 3))
    wbA = a16(78336, 16384)
    stgZ = a16(111104, 2 * L)

    @blk.vector
    def _(v):
        sdv.inc(v.memset(xin[:], 0.0))
    p2_dv_memset = sdv.n

    lld_h0 = lld.n

    @blk.gpsimd
    def _(g):
        for k in range(NB_DM):
            lld.inc(g.dma_start(wbA[:, k * 2048:(k + 1) * 2048],
                                d_win[k * 128:(k + 1) * 128, 0:2048]), 16)
    p2_w0 = lld.n

    fr0 = sfr.n
    pe_p2 = spe.n
    zsp = RB.start()
    ac_p2 = sac.n

    @blk.tensor
    def _(t):
        t.wait_ge(lld.h, p2_w0)
        t.wait_ge(sdv.h, p1_dv_end)
        t.wait_ge(sac.h, p1_ac_end)
        t.wait_ge(sdv.h, p2_dv_memset)
        for f in range(NB_DI):
            grp = (f % 2) * 4
            if f >= 2:
                t.wait_ge(sfr.h, fr0 + f - 1)
            for q in range(4):
                for k in range(NB_DM):
                    mm = t.matmul(pb[grp + q][:],
                             wbA[:, k * 2048 + f * 128:k * 2048 + f * 128 + 128],
                             xnt[:, k * L + q * 512:k * L + (q + 1) * 512],
                             start=(k == 0), stop=(k == NB_DM - 1))
            spe.inc(mm)
    pe_p2_h0 = spe.n

    @blk.vector
    def _(v):
        for f in range(NB_DI):
            v.wait_ge(spe.h, pe_p2 + f + 1)
            grp = (f % 2) * 4
            for q in range(4):
                ci = v.tensor_copy(
                    xin[:, f * (L + 3) + 3 + q * 512:f * (L + 3) + 3 + (q + 1) * 512],
                    pb[grp + q][:])
            sfr.inc(ci)

    dv_g0 = sdv.n
    dv_gred, dv_gsc, dv_gcol = dv_g0 + 1, dv_g0 + 2, dv_g0 + 3
    pe_g0 = spe.n
    pe_gsum, pe_gbc = pe_g0 + 1, pe_g0 + 2
    NQ1 = 8 * 128 * 4096

    @blk.vector
    def _(v):
        v.wait_ge(sac.h, pass1_ac_end)
        v.tensor_reduce(gtmp[:, 36:37], gtmp[:, 0:16], AX, ALU.add)
        sdv.inc(v.tensor_reduce(gtmp[:, 37:38], gtmp[:, 16:32], AX, ALU.add))
        v.wait_ge(spe.h, pe_gsum)
        v.tensor_scalar(gtmp[0:1, 38:39], pb[0][0:1, 0:1],
                        0.5 / NQ1, 0.5 * EPS_Q, ALU.mult, ALU.add)
        sdv.inc(v.tensor_scalar(gtmp[0:1, 39:40], pb[1][0:1, 0:1],
                                0.5 / NQ1, 0.5 * EPS_Q, ALU.mult, ALU.add))
        v.wait_ge(spe.h, pe_gbc)
        v.tensor_copy(gcol[:, 1:2], pb[0][:, 0:1])
        v.tensor_scalar(gcol[:, 2:3], pb[0][:, 0:1], -1.0, None, ALU.mult)
        v.tensor_copy(gcol[:, 3:4], pb[1][:, 0:1])
        sdv.inc(v.tensor_scalar(gcol[:, 4:5], pb[1][:, 0:1], -1.0, None,
                                ALU.mult))

    @blk.tensor
    def _(t):
        t.wait_ge(sdv.h, dv_gred)
        t.wait_ge(sfr.h, fr0 + 15)
        t.matmul(pb[0][0:1, 0:1], ones_c[:], gtmp[:, 36:37], start=True, stop=True)
        mm = t.matmul(pb[1][0:1, 0:1], ones_c[:], gtmp[:, 37:38], start=True, stop=True)
        spe.inc(mm)
        t.wait_ge(sdv.h, dv_gsc)
        t.matmul(pb[0][:, 0:1], ones_r[:], gtmp[0:1, 38:39], start=True, stop=True)
        mm = t.matmul(pb[1][:, 0:1], ones_r[:], gtmp[0:1, 39:40], start=True, stop=True)
        spe.inc(mm)

    @blk.gpsimd
    def _(g):
        g.wait_ge(spe.h, pe_p2_h0)
        for k in range(NB_DM):
            lld.inc(g.dma_start(wbA[:, k * 2048:(k + 1) * 2048],
                                d_win[k * 128:(k + 1) * 128, 2048:4096]), 16)
    p2_w1 = lld.n

    @blk.tensor
    def _(t):
        t.wait_ge(lld.h, p2_w1)
        t.wait_ge(sdv.h, dv_gcol)
        for i in range(NB_DI):
            f = NB_DI + i
            grp = (f % 2) * 4
            if f <= 17:
                t.wait_ge(sfr.h, fr0 + f - 1)
            else:
                t.wait_ge(sac.h, ac_p2 + f - 17)
            for q in range(4):
                for k in range(NB_DM):
                    mm = t.matmul(pb[grp + q][:],
                             wbA[:, k * 2048 + i * 128:k * 2048 + i * 128 + 128],
                             xnt[:, k * L + q * 512:k * L + (q + 1) * 512],
                             start=(k == 0), stop=(k == NB_DM - 1))
            spe.inc(mm)
    pe_p2_h1 = spe.n

    @blk.scalar
    def _(sc):
        for i in range(NB_DI):
            sc.wait_ge(spe.h, pe_p2_h0 + 2 + i + 1)
            if i >= 2:
                zsp.wait(sc, i - 2)
            grp = ((NB_DI + i) % 2) * 4
            for q in range(4):
                ai = sc.activation(stgZ[:, (i % 2) * L + q * 512:(i % 2) * L + (q + 1) * 512],
                                   pb[grp + q][:], AF.Silu)
            sac.inc(ai)

    @blk.sync
    def _(s):
        for i in range(NB_DI):
            s.wait_ge(sac.h, ac_p2 + i + 1)
            zsp.issue(s.dma_start(s_z[i], stgZ[:, (i % 2) * L:(i % 2) * L + L]), i)

    p2_fr_end = sfr.n

    conv64 = a16(65632, NB_DI * L)
    cvd = a16(131232, 2 * 512)

    dv_cvd = sdv.n
    pe_p3 = spe.n
    fr_p3 = sfr.n

    @blk.vector
    def _(v):
        v.wait_ge(spe.h, pe_p2_h1)
        v.wait_ge(sac.h, pass1_ac_end)
        for j in range(NB_DI):
            if j >= 2:
                v.wait_ge(spe.h, pe_p3 + j - 1)
                jj = j - 2
                if jj == 11:
                    zsp.wait(v, 14)
                    zsp.wait(v, 15)
                grp = (jj % 2) * 4
                for q in range(4):
                    ci = v.tensor_copy(conv64[:, jj * L + q * 512:jj * L + (q + 1) * 512],
                                       pb[grp + q][:])
                sfr.inc(ci)
            for k in range(4):
                ci = v.tensor_scalar(cvd[:, (j % 2) * 512 + k * 128:(j % 2) * 512 + (k + 1) * 128],
                                     eyem[:], t_dwt[:, j * 4 + k:j * 4 + k + 1], None, ALU.mult)
            sdv.inc(ci)
        for jj in (NB_DI - 2, NB_DI - 1):
            v.wait_ge(spe.h, pe_p3 + jj + 1)
            grp = (jj % 2) * 4
            for q in range(4):
                ci = v.tensor_copy(conv64[:, jj * L + q * 512:jj * L + (q + 1) * 512],
                                   pb[grp + q][:])
            sfr.inc(ci)

    @blk.tensor
    def _(t):
        t.wait_ge(sfr.h, p2_fr_end)
        t.wait_ge(sac.h, ac_p2 + NB_DI)
        for j in range(NB_DI):
            grp = (j % 2) * 4
            t.wait_ge(sdv.h, dv_cvd + j + 1)
            if j >= 2:
                t.wait_ge(sfr.h, fr_p3 + j - 1)
            for q in range(4):
                for k in range(4):
                    mm = t.matmul(pb[grp + q][:],
                             cvd[:, (j % 2) * 512 + k * 128:(j % 2) * 512 + (k + 1) * 128],
                             xin[:, j * (L + 3) + k + q * 512:j * (L + 3) + k + q * 512 + 512],
                             start=(k == 0), stop=(k == 3))
            spe.inc(mm)

    p3_fr_end = sfr.n
    p3_dv_end = sdv.n

    pw64 = a16(0, NB_DI * DI)
    xcs = a16(131232, 2 * L)

    @blk.gpsimd
    def _(g):
        g.wait_ge(spe.h, pe_p2_h1)
        for k in range(NB_DI):
            g.wait_ge(spe.h, pe_p3 + k + 1)
            lld.inc(g.dma_start(pw64[:, k * 2048:(k + 1) * 2048],
                                d_pw[k * 128:(k + 1) * 128, :]), 16)
    p4_w = lld.n

    pe_p4 = spe.n
    ac_p4 = sac.n
    xsp = RB.start()

    @blk.tensor
    def _(t):
        t.wait_ge(lld.h, p4_w)
        t.wait_ge(sfr.h, p3_fr_end)
        for f in range(NB_DI):
            grp = (f % 2) * 4
            if f >= 2:
                t.wait_ge(sac.h, ac_p4 + f - 1)
            for q in range(4):
                for k in range(NB_DI):
                    mm = t.matmul(pb[grp + q][:],
                             pw64[:, k * 2048 + f * 128:k * 2048 + f * 128 + 128],
                             conv64[:, k * L + q * 512:k * L + (q + 1) * 512],
                             start=(k == 0), stop=(k == NB_DI - 1))
            spe.inc(mm)

    @blk.scalar
    def _(sc):
        for f in range(NB_DI):
            sc.wait_ge(spe.h, pe_p4 + f + 1)
            if f >= 2:
                xsp.wait(sc, f - 2)
            for q in range(4):
                ai = sc.activation(xcs[:, (f % 2) * L + q * 512:(f % 2) * L + (q + 1) * 512],
                                   pb[(f % 2) * 4 + q][:], AF.Silu, bias=cbias[:, f:f + 1])
            sac.inc(ai)

    @blk.sync
    def _(s):
        for f in range(NB_DI):
            s.wait_ge(sac.h, ac_p4 + f + 1)
            xsp.issue(s.dma_start(s_xc[f], xcs[:, (f % 2) * L:(f % 2) * L + L]), f)

    ac_p4_end = sac.n

    qin2 = [a32(139424 + s * 4096, 1024) for s in range(3)]
    qout2 = [a16(151712 + s * 2048, 1024) for s in range(2)]
    NQC = 64

    dv_q2b = sdv.n
    rs2 = RS3.start()
    spq = RT.start()

    def q1done(c):
        return dv_q2b + c + 1

    def q2src(c):
        if c < 32:
            return d_w1t[(c // 4) * 128:(c // 4 + 1) * 128,
                         (c % 4) * 1024:(c % 4 + 1) * 1024]
        return d_w2t[(c - 32) * 128:(c - 31) * 128, :]

    def q2dst(c):
        if c < 32:
            return s_wq1[c // 4][:, (c % 4) * 1024:(c % 4 + 1) * 1024]
        return s_wq2[c - 32]

    @blk.vector
    def _(v):
        for c in range(NQC):
            thr, nthr = (1, 2) if c < 32 else (3, 4)
            rs2.wait(v, c)
            if c >= 2:
                spq.wait(v, c - 2)
            v.tensor_scalar(qout2[c % 2][:], qin2[c % 3][:],
                            gcol[:, nthr:nthr + 1], -1.0, ALU.is_gt, ALU.add)
            sdv.inc(v.scalar_tensor_tensor(qout2[c % 2][:], qin2[c % 3][:],
                                           gcol[:, thr:thr + 1], qout2[c % 2][:],
                                           ALU.is_ge, ALU.add))

    @blk.gpsimd
    def _(g):
        def spill(c):
            spq.issue(g.dma_start(q2dst(c), qout2[c % 2][:]), c)

        g.wait_ge(sac.h, pass1_ac_end)
        for c in range(NQC):
            if c >= 3:
                g.wait_ge(sdv.h, q1done(c - 3))
                spill(c - 3)
            rs2.issue(g.dma_start(qin2[c % 3][:], q2src(c)), c)
        for c in (NQC - 3, NQC - 2, NQC - 1):
            g.wait_ge(sdv.h, q1done(c))
            spill(c)
        for k in range(NB_DI):
            lcast.inc(g.dma_start(s_wo[k * 128:(k + 1) * 128, :],
                                  d_wout[k * 128:(k + 1) * 128, :]), 16)

    xcr = a16(106496, 2 * L)
    d016 = a16(8192, L)[0:64, :]
    bc16 = a16(12288, L)[0:16, :]
    p5t = a32(16384, L)
    stgD = a16(24576, 2 * L)

    pe_p5 = spe.n
    xcl = RA.start()

    @blk.sync
    def _(s):
        for k in range(NB_DI):
            xsp.wait(s, k)
            if k >= 2:
                s.wait_ge(spe.h, pe_p5 + k - 1)
            xcl.issue(s.dma_start(xcr[:, (k % 2) * L:(k % 2) * L + L], s_xc[k]), k)

    @blk.tensor
    def _(t):
        t.wait_ge(sac.h, ac_p4 + 15)
        for k in range(NB_DI):
            xcl.wait(t, k)
            for q in range(4):
                mm = t.matmul(pb[q][0:80, :],
                         wx16[:, k * 80:(k + 1) * 80],
                         xcr[:, (k % 2) * L + q * 512:(k % 2) * L + (q + 1) * 512],
                         start=(k == 0), stop=(k == NB_DI - 1))
            spe.inc(mm)
    pe_p5a = spe.n

    @blk.vector
    def _(v):
        v.wait_ge(spe.h, pe_p5a)
        for q in range(4):
            v.tensor_copy(d016[:, q * 512:(q + 1) * 512], pb[q][0:64, :])
            ci = v.tensor_copy(bc16[:, q * 512:(q + 1) * 512], pb[q][64:80, :])
        sdv.inc(ci)

    dv_p5bc = sdv.n
    pe_p5b = spe.n
    ac_p5b = sac.n
    dsp = RB.start()

    @blk.tensor
    def _(t):
        t.wait_ge(sdv.h, dv_p5bc)
        for f in range(NB_DI):
            grp = (f % 2) * 4
            if f >= 2:
                t.wait_ge(sac.h, ac_p5b + 2 * (f - 1))
            for q in range(4):
                mm = t.matmul(pb[grp + q][:],
                         wdt16[:, f * 128:(f + 1) * 128],
                         d016[:, q * 512:(q + 1) * 512],
                         start=True, stop=True)
            spe.inc(mm)
    pe_p5b_end = spe.n

    @blk.scalar
    def _(sc):
        RB.barrier(sc)
        for f in range(NB_DI):
            sc.wait_ge(spe.h, pe_p5b + f + 1)
            if f >= 2:
                dsp.wait(sc, f - 2)
            grp = (f % 2) * 4
            for q in range(4):
                ai = sc.activation(p5t[:, q * 512:(q + 1) * 512], pb[grp + q][:],
                                   AF.Exp, bias=bdt[:, f:f + 1])
            sac.inc(ai)
            for q in range(4):
                ai = sc.activation(stgD[:, (f % 2) * L + q * 512:(f % 2) * L + (q + 1) * 512],
                                   p5t[:, q * 512:(q + 1) * 512], AF.Ln, bias=1.0)
            sac.inc(ai)

    ac_p5_end = sac.n

    @blk.sync
    def _(s):
        for f in range(NB_DI):
            s.wait_ge(sac.h, ac_p5b + 2 * (f + 1))
            dsp.issue(s.dma_start(s_dl[f], stgD[:, (f % 2) * L:(f % 2) * L + L]), f)

    brep = a16(40960, DS * L)
    crep = a16(73728, DS * L)

    pe_p5c = spe.n
    dv_rep0 = sdv.n

    @blk.tensor
    def _(t):
        t.wait_ge(sac.h, ac_p5_end)
        for n in range(2 * DS):
            grp = (n % 2) * 4
            if n >= 2:
                t.wait_ge(sdv.h, dv_rep0 + n - 1)
            for q in range(4):
                mm = t.matmul(pb[grp + q][:], ind16[:, n * 128:(n + 1) * 128],
                         bc16[:, q * 512:(q + 1) * 512], start=True, stop=True)
            spe.inc(mm)
    pe_p5c_end = spe.n

    @blk.vector
    def _(v):
        for n in range(2 * DS):
            v.wait_ge(spe.h, pe_p5c + n + 1)
            grp = (n % 2) * 4
            dst = brep if n < DS else crep
            nn = n % DS
            for q in range(4):
                ci = v.tensor_copy(dst[:, nn * L + q * 512:nn * L + (q + 1) * 512],
                                   pb[grp + q][:])
            sdv.inc(ci)

    dv_p5_end = sdv.n

    dl_s = [a16(0, L), a16(4096, L)]
    xc_s = [a16(8192, L), a16(12288, L)]
    zj_s = [a16(16384, L), a16(20480, L)]
    a_s = [a16(24576, L), a16(28672, L)]
    uj_s = a16(32768, L)
    b_s = a16(36864, L)
    h_s = a16(106496, L)
    xcd_s = a16(110592, L)
    ty2_s = a16(114688, L)
    hc_s = [a16(118784 + n * 4096, L) for n in range(DS)]
    wq1r = [a16(151552, 1024), a16(153600, 1024)]
    h1stg = a16(155808, L)

    ac_p6 = sac.n
    dv_p6 = sdv.n
    pe_p6 = spe.n
    dl6 = RA.start()
    xc6 = RC.start()
    z6 = RD.start()
    ysp = RE.start()
    rg = RG.start()
    h8 = RB.start()

    def dv_scan(i):
        j, n = divmod(i, DS)
        return dv_p6 + 31 * j + 3 * n + 3

    def dv_hc(i):
        j, n = divmod(i, DS)
        return dv_p6 + 31 * j + 3 * n + 4

    def dv_t1(j):
        return dv_p6 + 31 * j + 30

    def dv_y2(j):
        return dv_p6 + 31 * (j + 1)

    def pe_sumb(j):
        return pe_p6 + 3 * j + 1

    def ffn1_done(f):
        return pe_p6 + 3 * (f // 2) + 2 + (f % 2)

    sac_a = {}
    sac_relu = [None] * NB_FF

    @blk.scalar
    def _(sc):
        def emit_relu(f):
            sc.wait_ge(spe.h, ffn1_done(f))
            if f >= 1:
                h8.wait(sc, f - 1)
            for q in range(4):
                ai = sc.activation(h1stg[:, q * 512:(q + 1) * 512],
                                   pb[4 + q][:], AF.Relu, bias=b1t[:, f:f + 1])
            sac_relu[f] = sac.inc(ai)

        sc.wait_ge(sdv.h, dv_p5_end)
        h8.barrier_base(sc)
        for j in range(NB_DI):
            dl6.wait(sc, j)
            for n in range(DS):
                i = j * DS + n
                if i >= 2:
                    sc.wait_ge(sdv.h, dv_scan(i - 2))
                sac_a[i] = sac.inc(sc.activation(
                    a_s[i % 2], dl_s[j % 2], AF.Exp,
                    scale=aneg[:, j * DS + n:j * DS + n + 1]))
                if n == 1 and j >= 1:
                    emit_relu(2 * j - 2)
                if n == 4 and j >= 1:
                    emit_relu(2 * j - 1)
        emit_relu(2 * NB_DI - 2)
        emit_relu(2 * NB_DI - 1)

    @blk.sync
    def _(s):
        def loads(j):
            dsp.wait(s, j)
            dl6.issue(s.dma_start(dl_s[j % 2], s_dl[j]), j)
            xc6.issue(s.dma_start(xc_s[j % 2], s_xc[j]), j)
            z6.issue(s.dma_start(zj_s[j % 2], s_z[j]), j)

        s.wait_ge(spe.h, pe_p5c_end)
        spq.wait_all(s)
        loads(0)
        loads(1)
        for j in range(NB_DI):
            if j >= 1:
                s.wait_ge(sdv.h, dv_y2(j - 1))
                ysp.issue(s.dma_start(s_y2[j - 1].rearrange("t p c -> p t c"),
                                      ty2_s.rearrange("p (t c) -> p t c", t=NT)), j - 1)
                if j + 1 < NB_DI:
                    loads(j + 1)
            for f in (2 * j - 2, 2 * j - 1):
                if 0 <= f < NB_FF:
                    s.wait_ge(sac.h, sac_relu[f])
                    h8.issue(s.dma_start(
                        s_h1[f].rearrange("t p c -> p t c"),
                        h1stg[:].rearrange("p (t c) -> p t c", t=NT)), f)
            for f in (2 * j, 2 * j + 1):
                if f >= 2:
                    s.wait_ge(spe.h, ffn1_done(f - 2))
                rg.issue(s.dma_start(
                    wq1r[f % 2].rearrange("p (k c) -> p k c", k=NB_DM),
                    s_wq1[:, :, f * 128:(f + 1) * 128].rearrange("k p c -> p k c")), f)
        for f in (NB_FF - 2, NB_FF - 1):
            s.wait_ge(sac.h, sac_relu[f])
            h8.issue(s.dma_start(
                s_h1[f].rearrange("t p c -> p t c"),
                h1stg[:].rearrange("p (t c) -> p t c", t=NT)), f)
        s.wait_ge(sdv.h, dv_y2(NB_DI - 1))
        ysp.issue(s.dma_start(s_y2[NB_DI - 1].rearrange("t p c -> p t c"),
                              ty2_s.rearrange("p (t c) -> p t c", t=NT)), NB_DI - 1)

    @blk.vector
    def _(v):
        h8.barrier_base(v)
        spq.wait_all(v)
        for j in range(NB_DI):
            dl6.wait(v, j)
            xc6.wait(v, j)
            if j >= 1:
                v.wait_ge(spe.h, pe_sumb(j - 1))
            sdv.inc(v.tensor_tensor(uj_s, dl_s[j % 2], xc_s[j % 2], ALU.mult))
            for n in range(DS):
                i = j * DS + n
                v.wait_ge(sac.h, sac_a[i])
                sdv.inc(v.tensor_tensor(b_s, uj_s, brep[:, n * L:(n + 1) * L], ALU.mult))
                sdv.inc(v.tensor_tensor_scan(h_s, a_s[i % 2], b_s, 0.0, ALU.mult, ALU.add))
                sdv.inc(v.tensor_tensor(hc_s[n], h_s, crep[:, n * L:(n + 1) * L], ALU.mult))
            xc6.wait(v, j)
            sdv.inc(v.tensor_scalar(xcd_s, xc_s[j % 2], dtab[:, j:j + 1], None, ALU.mult))
            v.wait_ge(spe.h, pe_sumb(j))
            if j >= 1:
                ysp.wait(v, j - 1)
            for q in range(4):
                sdv.inc(v.tensor_tensor(ty2_s[:, q * 512:(q + 1) * 512], pb[q][:],
                                        xcd_s[:, q * 512:(q + 1) * 512], ALU.add))
            z6.wait(v, j)
            sdv.inc(v.tensor_tensor(ty2_s, ty2_s, zj_s[j % 2], ALU.mult))
    assert sdv.n == dv_p6 + 31 * NB_DI

    @blk.tensor
    def _(t):
        for j in range(NB_DI):
            if j >= 1:
                t.wait_ge(sdv.h, dv_t1(j - 1))
            for n in range(DS):
                t.wait_ge(sdv.h, dv_hc(j * DS + n))
                for q in range(4):
                    mm = t.matmul(pb[q][:], eye16[:],
                                  hc_s[n][:, q * 512:(q + 1) * 512],
                                  start=(n == 0), stop=(n == DS - 1))
            spe.inc(mm)
            for f in (2 * j, 2 * j + 1):
                rg.wait(t, f)
                if f >= 1:
                    t.wait_ge(sac.h, sac_relu[f - 1])
                for q4 in range(4):
                    for k in range(NB_DM):
                        mm = t.matmul(pb[4 + q4][:],
                                      wq1r[f % 2][:, k * 128:(k + 1) * 128],
                                      xnt[:, k * L + q4 * 512:k * L + (q4 + 1) * 512],
                                      start=(k == 0), stop=(k == NB_DM - 1))
                spe.inc(mm)
    pe_p6_end = spe.n
    assert pe_p6_end == pe_p6 + 3 * NB_DI

    p6_dv_end = sdv.n

    wo16 = a16(0, NB_DI * DM)
    wq2f = a16(32768, NB_FF * DM)
    lr6 = [a16(98304 + s * 1024, 512) for s in range(6)]
    xre2 = [a32(110592 + s * 8192, 2 * DM) for s in range(2)]
    outst2 = [a32(126976 + s * 8192, 2 * DM) for s in range(2)]
    b2bc = a32(143360, DM)

    NKK = NB_DI + NB_FF
    pe_pt0 = spe.n
    dv_pt0 = sdv.n
    dv_b2bc = dv_pt0 + 1

    def pe_kk(i):
        return pe_pt0 + 2 + i

    def dv_evac(g8):
        return dv_pt0 + 2 + g8

    def lhs_consumed(c):
        g, qq = c // 24, (c % 24) // 2
        return pe_kk(g * NKK + 4 * qq + 3)

    wring = RW.start()
    lr = RL.start()
    xrl = RC.start()
    org = RD.start()

    @blk.sync
    def _(s):
        s.wait_ge(sdv.h, p6_dv_end)
        s.wait_ge(spe.h, pe_p6_end)
        s.wait_ge(lcast.h, 16 * NB_DI)
        h8.wait_all(s)
        ysp.wait_all(s)
        for g8 in range(8):
            if g8 >= 2:
                s.wait_ge(sdv.h, dv_evac(g8 - 2))
            xrl.issue(s.dma_start(
                xre2[g8 % 2].rearrange("p (t d) -> p t d", t=2),
                d_x.rearrange("(t p) d -> p t d", p=128)[:, 2 * g8:2 * g8 + 2, :]), g8)
            for q4 in range(NKK // 4):
                q = g8 * (NKK // 4) + q4
                if g8 == 0:
                    for kk in range(4 * q4, 4 * q4 + 4):
                        if kk >= 12:
                            s.wait_ge(spe.h, pe_kk(kk - 12))
                        if kk < NB_DI:
                            wring.issue(s.dma_start(
                                wo16[:, kk * 1024:(kk + 1) * 1024],
                                s_wo[kk * 128:(kk + 1) * 128, :]), kk)
                        else:
                            wring.issue(s.dma_start(
                                wq2f[:, (kk - NB_DI) * 1024:(kk - NB_DI + 1) * 1024],
                                s_wq2[kk - NB_DI]), kk)
                j0 = 4 * q4
                srcb = (s_y2[j0:j0 + 4] if j0 < NB_DI else s_h1[j0 - NB_DI:j0 - NB_DI + 4])
                for tt in range(2):
                    c = q * 2 + tt
                    if c >= 6:
                        s.wait_ge(spe.h, lhs_consumed(c - 6))
                    lr.issue(s.dma_start(
                        lr6[c % 6].rearrange("p (j c) -> p j c", j=4),
                        srcb[:, 2 * g8 + tt].rearrange("j p c -> p j c")), c)
            if g8 >= 1:
                s.wait_ge(sdv.h, dv_evac(g8 - 1))
                org.issue(s.dma_start(
                    d_out.rearrange("(t p) d -> p t d", p=128)[:, 2 * (g8 - 1):2 * g8, :],
                    outst2[(g8 - 1) % 2].rearrange("p (t d) -> p t d", t=2)), g8 - 1)
        s.wait_ge(sdv.h, dv_evac(7))
        org.issue(s.dma_start(
            d_out.rearrange("(t p) d -> p t d", p=128)[:, 14:16, :],
            outst2[1].rearrange("p (t d) -> p t d", t=2)), 7)
        org.wait_all(s)

    @blk.tensor
    def _(t):
        t.wait_ge(sld.h, p0_sld)
        t.wait_ge(sac.h, sac_relu[NB_FF - 1])
        mm = t.matmul(pb[4][:], ones_r[:], b2row[:, 0:512], start=True, stop=True)
        mm = t.matmul(pb[5][:], ones_r[:], b2row[:, 512:1024], start=True, stop=True)
        spe.inc(mm)
        t.wait_ge(sdv.h, p6_dv_end)
        for g8 in range(8):
            B = (g8 % 2) * 4
            if g8 == 1:
                t.wait_ge(sdv.h, dv_b2bc)
            if g8 >= 2:
                t.wait_ge(sdv.h, dv_evac(g8 - 2))
            for kk in range(NKK):
                q = (g8 * NKK + kk) // 4
                if g8 == 0:
                    wring.wait(t, kk)
                if kk % 4 == 0:
                    lr.wait(t, 2 * q)
                    lr.wait(t, 2 * q + 1)
                w = wo16 if kk < NB_DI else wq2f
                co = (kk if kk < NB_DI else kk - NB_DI) * 1024
                lo = (kk % 4) * 128
                for tt in range(2):
                    for h in range(2):
                        mm = t.matmul(pb[B + tt * 2 + h][:],
                                      lr6[(2 * q + tt) % 6][:, lo:lo + 128],
                                      w[:, co + h * 512:co + (h + 1) * 512],
                                      start=(kk == 0), stop=(kk == NKK - 1))
                spe.inc(mm)

    @blk.vector
    def _(v):
        v.wait_ge(spe.h, pe_pt0 + 1)
        v.tensor_copy(b2bc[:, 0:512], pb[4][:])
        ci = v.tensor_copy(b2bc[:, 512:1024], pb[5][:])
        sdv.inc(ci)
        for g8 in range(8):
            B = (g8 % 2) * 4
            v.wait_ge(spe.h, pe_kk(g8 * NKK + NKK - 1))
            xrl.wait(v, g8)
            if g8 >= 2:
                org.wait(v, g8 - 2)
            xs = xre2[g8 % 2]
            v.tensor_tensor(xs[:, 0:1024], xs[:, 0:1024], b2bc[:], ALU.add)
            v.tensor_tensor(xs[:, 1024:2048], xs[:, 1024:2048], b2bc[:], ALU.add)
            for tt in range(2):
                for h in range(2):
                    c0 = tt * 1024 + h * 512
                    ci = v.tensor_tensor(outst2[g8 % 2][:, c0:c0 + 512],
                                         pb[B + tt * 2 + h][:],
                                         xs[:, c0:c0 + 512], ALU.add)
            sdv.inc(ci)

    if dbg:
        d_dbg = {}
        for nm, st_t, shp in [("dbg_z", s_z, [NB_DI, 128, L]),
                              ("dbg_xc", s_xc, [NB_DI, 128, L]),
                              ("dbg_dl", s_dl, [NB_DI, 128, L]),
                              ("dbg_y2", s_y2, [NB_DI, NT, 128, 128]),
                              ("dbg_wq1", s_wq1, [NB_DM, 128, DFF]),
                              ("dbg_h1", s_h1, [NB_FF, NT, 128, 128])]:
            d_dbg[nm] = (nc.dram_tensor(nm, shp, BF16, kind="ExternalOutput").ap(), st_t)
        d_xnt_o = nc.dram_tensor("dbg_xnt", [128, NB_DM * L], BF16, kind="ExternalOutput").ap()
        d_gcol_o = nc.dram_tensor("dbg_gcol", [128, 6], F32, kind="ExternalOutput").ap()
        d_gtmp_o = nc.dram_tensor("dbg_gtmp", [128, 40], F32, kind="ExternalOutput").ap()

        @blk.sync
        def _(s):
            org.wait_all(s)
            for nm, (dst, st_t) in d_dbg.items():
                sld.inc(s.dma_start(dst, st_t), 16)
            sld.inc(s.dma_start(d_xnt_o, xnt[:]), 16)
            sld.inc(s.dma_start(d_gcol_o, gcol[:]), 16)
            sld.inc(s.dma_start(d_gtmp_o, gtmp[:]), 16)
            s.wait_ge(sld.h, sld.n)

    blk.__exit__(None, None, None)
    for cm in reversed(list(live.values())):
        cm.__exit__(None, None, None)
    nc.finalize()
    return nc


def kernel(**inputs):
    xfull = np.ascontiguousarray(inputs["L0_attn_x"], dtype=np.float32)
    w_in = np.ascontiguousarray(inputs["L0_attn_W_in"], dtype=np.float32)
    pw = np.ascontiguousarray(inputs["L0_attn_pw_kernel"], dtype=np.float32)
    w_x = np.ascontiguousarray(
        np.asarray(inputs["L0_attn_W_x"], dtype=np.float32)
        .reshape(NB_DI, 128, 80).transpose(1, 0, 2).reshape(128, NB_DI * 80))
    w_dt = np.ascontiguousarray(inputs["L0_attn_W_dt"], dtype=np.float32)
    w_out = np.ascontiguousarray(inputs["L0_attn_W_out"], dtype=np.float32)
    w1t = np.ascontiguousarray(np.asarray(inputs["L0_ffn_W1"], dtype=np.float32).T)
    w2t = np.ascontiguousarray(np.asarray(inputs["L0_ffn_W2"], dtype=np.float32).T)
    def blocky(vec, nb):
        return np.ascontiguousarray(
            np.asarray(vec, dtype=np.float32).reshape(nb, 128).T)

    dwt = np.ascontiguousarray(
        np.asarray(inputs["L0_attn_dw_kernel"], dtype=np.float32).T
        .reshape(NB_DI, 128, 4).transpose(1, 0, 2).reshape(128, NB_DI * 4))
    alog = np.ascontiguousarray(
        np.asarray(inputs["L0_attn_A_log"], dtype=np.float32)
        .reshape(NB_DI, 128, DS).transpose(1, 0, 2).reshape(128, NB_DI * DS))
    dvec = blocky(inputs["L0_attn_D"], NB_DI)
    cb = blocky(inputs["L0_attn_conv_bias"], NB_DI)
    bdtv = blocky(inputs["L0_attn_b_dt"], NB_DI)
    b1 = blocky(inputs["L0_ffn_b1"], NB_FF)
    b2 = np.ascontiguousarray(np.asarray(inputs["L0_ffn_b2"], dtype=np.float32).reshape(1, DM))

    eye16 = np.eye(128, dtype=ml_dtypes.bfloat16)
    eyem = np.eye(128, dtype=np.float32)
    ind = np.zeros((16, 16 * 128), dtype=ml_dtypes.bfloat16)
    for ssel in range(16):
        ind[ssel, ssel * 128:(ssel + 1) * 128] = 1.0
    ones_r = np.ones((1, 128), dtype=np.float32)
    ones_c = np.ones((128, 1), dtype=np.float32)

    nc = build_core()
    shared = dict(w_in=w_in, pw=pw, w_x=w_x, w_dt=w_dt, w_out=w_out, w1t=w1t,
                  w2t=w2t, dwt=dwt, a_log=alog, d_vec=dvec, conv_bias=cb,
                  b_dt=bdtv, b1=b1, b2=b2, eye16=eye16, eyem=eyem, ind=ind,
                  ones_r=ones_r, ones_c=ones_c)
    in_maps = [dict(shared, x=np.ascontiguousarray(xfull[b])) for b in range(8)]
    res = run_bass_kernel_spmd(nc, in_maps, core_ids=list(range(8)))
    return np.stack([res.results[b]["out"] for b in range(8)], axis=0)


# version 25
# speedup vs baseline: 1.0227x; 1.0004x over previous
import numpy as np
import ml_dtypes
import concourse.bass as bass
import concourse.mybir as mybir
from concourse.bass_utils import run_bass_kernel_spmd

F32 = mybir.dt.float32
BF16 = mybir.dt.bfloat16
I32 = mybir.dt.int32
AF = mybir.ActivationFunctionType
ALU = mybir.AluOpType
AX = mybir.AxisListType.X

L = 2048
DM = 1024
DI = 2048
DS = 8
DTR = 64
DFF = 4096
NB_DM = DM // 128
NB_DI = DI // 128
NB_FF = DFF // 128
NT = L // 128
EPS_LN = 1e-5
EPS_Q = 1e-5


class Sem:

    def __init__(self, nc, name):
        self._cm = nc.semaphore(name)
        self.h = self._cm.__enter__()
        self.n = 0

    def inc(self, instr, k=1):
        instr.then_inc(self.h, k)
        self.n += k
        return self.n


class Ring:

    def __init__(self, nc, name, slots=2):
        self.sems = [Sem(nc, f"{name}{i}") for i in range(slots)]
        self.slots = slots

    def start(self):
        return Stream(self)

    def barrier(self, eng):
        for s in self.sems:
            eng.wait_ge(s.h, s.n)


class Stream:
    def __init__(self, ring):
        self.ring = ring
        self.base = [s.n for s in ring.sems]

    def issue(self, instr, idx):
        self.ring.sems[idx % self.ring.slots].inc(instr, 16)

    def wait(self, eng, idx):
        p = idx % self.ring.slots
        eng.wait_ge(self.ring.sems[p].h, self.base[p] + 16 * (idx // self.ring.slots + 1))

    def wait_all(self, eng):
        for s in self.ring.sems:
            eng.wait_ge(s.h, s.n)

    def barrier_base(self, eng):
        for s, b in zip(self.ring.sems, self.base):
            eng.wait_ge(s.h, b)


def build_core(dbg=False):
    nc = bass.Bass("TRN2", target_bir_lowering=False)

    d_x = nc.dram_tensor("x", [L, DM], F32, kind="ExternalInput").ap()
    d_win = nc.dram_tensor("w_in", [DM, 2 * DI], F32, kind="ExternalInput").ap()
    d_pw = nc.dram_tensor("pw", [DI, DI], F32, kind="ExternalInput").ap()
    d_wx = nc.dram_tensor("w_x", [128, NB_DI * 80], F32, kind="ExternalInput").ap()
    d_wdt = nc.dram_tensor("w_dt", [DTR, DI], F32, kind="ExternalInput").ap()
    d_wout = nc.dram_tensor("w_out", [DI, DM], F32, kind="ExternalInput").ap()
    d_w1t = nc.dram_tensor("w1t", [DM, DFF], F32, kind="ExternalInput").ap()
    d_w2t = nc.dram_tensor("w2t", [DFF, DM], F32, kind="ExternalInput").ap()
    d_dwt = nc.dram_tensor("dwt", [128, NB_DI * 4], F32, kind="ExternalInput").ap()
    d_alog = nc.dram_tensor("a_log", [128, NB_DI * DS], F32, kind="ExternalInput").ap()
    d_dvec = nc.dram_tensor("d_vec", [128, NB_DI], F32, kind="ExternalInput").ap()
    d_cb = nc.dram_tensor("conv_bias", [128, NB_DI], F32, kind="ExternalInput").ap()
    d_bdt = nc.dram_tensor("b_dt", [128, NB_DI], F32, kind="ExternalInput").ap()
    d_b1 = nc.dram_tensor("b1", [128, NB_FF], F32, kind="ExternalInput").ap()
    d_b2 = nc.dram_tensor("b2", [1, DM], F32, kind="ExternalInput").ap()
    d_eye16 = nc.dram_tensor("eye16", [128, 128], BF16, kind="ExternalInput").ap()
    d_eyem = nc.dram_tensor("eyem", [128, 128], F32, kind="ExternalInput").ap()
    d_ind = nc.dram_tensor("ind", [16, 16 * 128], BF16, kind="ExternalInput").ap()
    d_ones_r = nc.dram_tensor("ones_r", [1, 128], F32, kind="ExternalInput").ap()
    d_ones_c = nc.dram_tensor("ones_c", [128, 1], F32, kind="ExternalInput").ap()

    d_out = nc.dram_tensor("out", [L, DM], F32, kind="ExternalOutput").ap()

    s_z = nc.dram_tensor("s_z", [NB_DI, 128, L], BF16, kind="Internal").ap()
    s_xc = nc.dram_tensor("s_xc", [NB_DI, 128, L], BF16, kind="Internal").ap()
    s_dl = nc.dram_tensor("s_dl", [NB_DI, 128, L], BF16, kind="Internal").ap()
    s_y2 = nc.dram_tensor("s_y2", [NB_DI, NT, 128, 128], BF16, kind="Internal").ap()
    s_h1 = nc.dram_tensor("s_h1", [NB_FF, NT, 128, 128], BF16, kind="Internal").ap()
    s_wq1 = nc.dram_tensor("s_wq1", [NB_DM, 128, DFF], BF16, kind="Internal").ap()
    s_wq2 = nc.dram_tensor("s_wq2", [NB_FF, 128, DM], BF16, kind="Internal").ap()
    s_wo = nc.dram_tensor("s_wo", [DI, DM], BF16, kind="Internal").ap()

    live = {}
    order = []

    def sb(name, shape, dt):
        cm = nc.sbuf_tensor(name, shape, dt)
        t = cm.__enter__()
        live[name] = cm
        order.append(name)
        return t

    def ps(name, shape, dt):
        cm = nc.psum_tensor(name, shape, dt)
        t = cm.__enter__()
        live[name] = cm
        return t

    def free(*names):
        for name in sorted(names, key=order.index, reverse=True):
            live.pop(name).__exit__(None, None, None)

    cpool = sb("cpool", [128, 8], F32)
    eye16 = sb("t_eye16", [128, 128], BF16)
    eyem = sb("t_eyem", [128, 128], F32)
    ind16 = sb("t_ind16", [16, 16 * 128], BF16)
    ones_r = sb("t_onesr", [1, 128], F32)
    ones_c = sb("t_onesc", [128, 1], F32)
    aneg = sb("aneg", [128, NB_DI * DS], F32)
    dtab = sb("dtab", [128, NB_DI], F32)
    cbias = sb("cbias", [128, NB_DI], F32)
    bdt = sb("bdt", [128, NB_DI], F32)
    b1t = sb("b1t", [128, NB_FF], F32)
    b2row = sb("b2row", [1, DM], F32)
    gtmp = sb("gtmp", [128, 40], F32)
    gcol = sb("gcol", [128, 6], F32)
    t_dwt = sb("t_dwt", [128, NB_DI * 4], F32)
    wx16 = sb("wx16", [128, NB_DI * 80], BF16)
    wdt16 = sb("wdt16", [64, DI], BF16)
    xnt = sb("xnt", [128, NB_DM * L], BF16)
    arena = sb("arena", [128, 79952], BF16)
    pb = [ps(f"pb{i}", [128, 512], F32) for i in range(8)]
    pt_t = [pb[6][:, 0:64].bitcast(BF16), pb[7][:, 0:64].bitcast(BF16)]

    ar32_all = arena[:].bitcast(F32)
    ar_i32_all = arena[:].bitcast(I32)

    def a16(off_b, cols):
        assert off_b % 2 == 0
        return arena[:, off_b // 2:off_b // 2 + cols]

    def a32(off_b, cols):
        assert off_b % 4 == 0
        return ar32_all[:, off_b // 4:off_b // 4 + cols]

    def ai32(off_b, cols):
        assert off_b % 4 == 0
        return ar_i32_all[:, off_b // 4:off_b // 4 + cols]

    sld = Sem(nc, "sld")
    lld = Sem(nc, "lld")
    spe = Sem(nc, "spe")
    sdv = Sem(nc, "sdv")
    sac = Sem(nc, "sac")
    sfr = Sem(nc, "sfr")
    sgp = Sem(nc, "sgp")
    lcast = Sem(nc, "lcast")
    RA = Ring(nc, "rA")
    RB = Ring(nc, "rB")
    RC = Ring(nc, "rC")
    RD = Ring(nc, "rD")
    RE = Ring(nc, "rE", slots=1)
    RF = Ring(nc, "rF", slots=1)
    RG = Ring(nc, "rG")
    RH = Ring(nc, "rH", slots=1)
    RL = Ring(nc, "rL", slots=6)
    RW = Ring(nc, "rW", slots=12)
    RS = Ring(nc, "rS", slots=2)
    RS3 = Ring(nc, "rS3", slots=3)
    RT = Ring(nc, "rT", slots=2)
    RX = Ring(nc, "rX", slots=4)

    blk = nc.Block().__enter__()

    @blk.sync
    def _(s):
        for dst, src in [
            (eye16[:], d_eye16), (eyem[:], d_eyem), (ind16[:], d_ind),
            (ones_r[:], d_ones_r), (ones_c[:], d_ones_c), (b2row[:], d_b2),
            (aneg[:], d_alog), (dtab[:], d_dvec), (cbias[:], d_cb),
            (bdt[:], d_bdt), (b1t[:], d_b1), (t_dwt[:], d_dwt),
        ]:
            sld.inc(s.dma_start(dst, src), 16)
    p0_sld = sld.n

    @blk.gpsimd
    def _(g):
        lld.inc(g.dma_start(wx16[:], d_wx), 16)
        lld.inc(g.dma_start(wdt16[:], d_wdt), 16)

    @blk.vector
    def _(v):
        consts = [0.0, 1.0, -1.0, EPS_LN, -0.5, 1.0 / DM, 0.5, 0.0]
        for i, c in enumerate(consts):
            ci = v.memset(cpool[:, i:i + 1], c)
        sdv.inc(ci)

    nc.const_aps.register(F32, -1.0, cpool[:, 2:3])
    nc.const_aps.register(F32, EPS_LN, cpool[:, 3:4])
    nc.const_aps.register(F32, -0.5, cpool[:, 4:5])
    nc.const_aps.register(F32, 1.0 / DM, cpool[:, 5:6])
    nc.const_aps.register(F32, 0.5, cpool[:, 6:7])

    @blk.scalar
    def _(sc):
        sc.wait_ge(sld.h, p0_sld)
        sac.inc(sc.activation(aneg[:], aneg[:], AF.Exp))

    @blk.vector
    def _(v):
        v.wait_ge(sac.h, sac.n)
        sdv.inc(v.tensor_scalar(aneg[:], aneg[:], -1.0, None, ALU.mult))

    xtok = a32(0, NT * DM)
    lnsq = a32(65536, 2 * DM)
    lnst = a32(73728, 128)
    lnxn = a16(74240, 2 * DM)
    qp1 = [a32(111104 + s * 16384, 4096) for s in range(2)]
    qtr = a32(143872, 2048)

    px = RX.start()

    @blk.sync
    def _(s):
        for g in range(4):
            px.issue(s.dma_start(
                xtok[:, g * 4 * DM:(g + 1) * 4 * DM].rearrange(
                    "p (t d) -> p t d", t=4),
                d_x.rearrange("(t p) d -> p t d", p=128)[:, 4 * g:4 * g + 4, :]),
                g)

    rs1 = RS.start()

    def q1load(sc, slot, c):
        if c < 8:
            return sc.dma_start(qp1[slot][:], d_w1t[c * 128:(c + 1) * 128, :])
        k = c - 8
        return sc.dma_start(
            qp1[slot][:].rearrange("p (a d) -> p a d", a=4),
            d_w2t[k * 512:(k + 1) * 512, :].rearrange("(a p) d -> p a d",
                                                      p=128))

    @blk.scalar
    def _(sc):
        rs1.issue(q1load(sc, 0, 0), 0)
        rs1.issue(q1load(sc, 1, 1), 1)

    ac0 = sac.n
    dv0 = sdv.n

    @blk.scalar
    def _(sc):
        sc.wait_ge(sdv.h, dv0)
        for t in range(NT):
            if t % 4 == 0:
                px.wait(sc, t // 4)
            if t >= 2:
                sc.wait_ge(sac.h, ac0 + t - 1)
            st = lnst[:, t * 8:t * 8 + 8]
            sac.inc(sc.activation(lnsq[:, (t % 2) * DM:(t % 2 + 1) * DM],
                                  xtok[:, t * DM:(t + 1) * DM], AF.Square,
                                  accum_out=st[:, 1:2]))

    @blk.vector
    def _(v):
        v.wait_ge(sac.h, ac0 + NT)
        for t in range(NT):
            v.tensor_reduce(lnst[:, t * 8:t * 8 + 1], xtok[:, t * DM:(t + 1) * DM],
                            AX, ALU.add)
        for t in range(NT):
            v.tensor_scalar(lnst[:, t * 8 + 2:t * 8 + 3], lnst[:, t * 8:t * 8 + 1],
                            1.0 / DM, None, ALU.mult)
        for t in range(NT):
            v.tensor_scalar(lnst[:, t * 8 + 4:t * 8 + 5], lnst[:, t * 8:t * 8 + 1],
                            -1.0 / DM, None, ALU.mult)
        for t in range(NT):
            v.tensor_scalar(lnst[:, t * 8 + 3:t * 8 + 4], lnst[:, t * 8 + 1:t * 8 + 2],
                            1.0 / DM, None, ALU.mult)
        for t in range(NT):
            v.tensor_tensor(lnst[:, t * 8 + 5:t * 8 + 6], lnst[:, t * 8 + 2:t * 8 + 3],
                            lnst[:, t * 8 + 4:t * 8 + 5], ALU.mult)
        for t in range(NT):
            sdv.inc(v.tensor_tensor(lnst[:, t * 8 + 5:t * 8 + 6],
                                    lnst[:, t * 8 + 3:t * 8 + 4],
                                    lnst[:, t * 8 + 5:t * 8 + 6], ALU.add))

    dv_var = sdv.n
    ac1 = sac.n

    @blk.scalar
    def _(sc):
        sc.wait_ge(sdv.h, dv_var)
        for t in range(NT):
            sc.activation(lnst[:, t * 8 + 6:t * 8 + 7], lnst[:, t * 8 + 5:t * 8 + 6],
                          AF.Ln, bias=EPS_LN)
        for t in range(NT):
            sac.inc(sc.activation(lnst[:, t * 8 + 7:t * 8 + 8],
                                  lnst[:, t * 8 + 6:t * 8 + 7], AF.Exp, scale=-0.5))

    dv1 = sdv.n

    @blk.vector
    def _(v):
        v.wait_ge(sac.h, ac1 + NT)
        for t in range(NT):
            sdv.inc(v.tensor_tensor(lnst[:, t * 8 + 5:t * 8 + 6],
                                    lnst[:, t * 8 + 4:t * 8 + 5],
                                    lnst[:, t * 8 + 7:t * 8 + 8], ALU.mult))

    pe0 = spe.n
    apply_done = [None] * NT
    evac_done = [None] * (NT * NB_DM)

    @blk.vector
    def _(v):
        def evacs(tp):
            for d in range(0, NB_DM, 2):
                i = tp * NB_DM + d
                v.wait_ge(spe.h, pe0 + i + 1)
                evac_done[i] = ('dv', sdv.inc(
                    v.tensor_copy(xnt[:, d * L + tp * 128:d * L + (tp + 1) * 128],
                                  pt_t[i % 2][:, 0:128])))
        for t in range(NT):
            st = lnst[:, t * 8:t * 8 + 8]
            if t >= 2:
                v.wait_ge(spe.h, pe0 + (t - 1) * NB_DM)
            apply_done[t] = sdv.inc(
                v.tensor_scalar(lnxn[:, (t % 2) * DM:(t % 2 + 1) * DM],
                                xtok[:, t * DM:(t + 1) * DM],
                                st[:, 7:8], st[:, 5:6], ALU.mult, ALU.add))
            if t >= 1:
                evacs(t - 1)
        evacs(NT - 1)

    @blk.scalar
    def _(sc):
        for tp in range(NT):
            for d in range(1, NB_DM, 2):
                i = tp * NB_DM + d
                sc.wait_ge(spe.h, pe0 + i + 1)
                evac_done[i] = ('ac', sac.inc(
                    sc.activation(xnt[:, d * L + tp * 128:d * L + (tp + 1) * 128],
                                  pt_t[i % 2][:, 0:128], AF.Copy)))
    p1_ac_end = sac.n

    ac_q1 = sac.n

    @blk.scalar
    def _(sc):
        for c in range(16):
            rs1.wait(sc, c)
            sac.inc(sc.activation(qtr[:], qp1[c % 2][:, 0:2048], AF.Abs,
                                  accum_out=gtmp[:, 2 * c:2 * c + 1]))
            sc.wait_ge(sac.h, ac_q1 + 2 * c + 1)
            sac.inc(sc.activation(qtr[:], qp1[c % 2][:, 2048:4096], AF.Abs,
                                  accum_out=gtmp[:, 2 * c + 1:2 * c + 2]))
            if c + 2 < 16:
                sc.wait_ge(sac.h, ac_q1 + 2 * c + 2)
                rs1.issue(q1load(sc, c % 2, c + 2), c + 2)
    pass1_ac_end = sac.n

    @blk.tensor
    def _(t_):
        for t in range(NT):
            t_.wait_ge(sdv.h, apply_done[t])
            for d in range(NB_DM):
                i = t * NB_DM + d
                if i >= 2:
                    eng, cnt = evac_done[i - 2]
                    t_.wait_ge(sdv.h if eng == 'dv' else sac.h, cnt)
                spe.inc(t_.transpose(
                    pt_t[i % 2][:, 0:128],
                    lnxn[:, (t % 2) * DM + d * 128:(t % 2) * DM + (d + 1) * 128],
                    eye16[:]))

    p1_dv_end = sdv.n

    xin = a16(0, NB_DI * (L + 3))
    wbA = a16(78336, 16384)
    stgZ = a16(111104, 2 * L)

    @blk.vector
    def _(v):
        sdv.inc(v.memset(xin[:], 0.0))
    p2_dv_memset = sdv.n

    lld_h0 = lld.n

    @blk.gpsimd
    def _(g):
        for k in range(NB_DM):
            lld.inc(g.dma_start(wbA[:, k * 2048:(k + 1) * 2048],
                                d_win[k * 128:(k + 1) * 128, 0:2048]), 16)
    p2_w0 = lld.n

    fr0 = sfr.n
    pe_p2 = spe.n
    zsp = RB.start()
    ac_p2 = sac.n

    @blk.tensor
    def _(t):
        t.wait_ge(lld.h, p2_w0)
        t.wait_ge(sdv.h, p1_dv_end)
        t.wait_ge(sac.h, p1_ac_end)
        t.wait_ge(sdv.h, p2_dv_memset)
        for f in range(NB_DI):
            grp = (f % 2) * 4
            if f >= 2:
                t.wait_ge(sfr.h, fr0 + f - 1)
            for q in range(4):
                for k in range(NB_DM):
                    mm = t.matmul(pb[grp + q][:],
                             wbA[:, k * 2048 + f * 128:k * 2048 + f * 128 + 128],
                             xnt[:, k * L + q * 512:k * L + (q + 1) * 512],
                             start=(k == 0), stop=(k == NB_DM - 1))
            spe.inc(mm)
    pe_p2_h0 = spe.n

    @blk.vector
    def _(v):
        for f in range(NB_DI):
            v.wait_ge(spe.h, pe_p2 + f + 1)
            grp = (f % 2) * 4
            for q in range(4):
                ci = v.tensor_copy(
                    xin[:, f * (L + 3) + 3 + q * 512:f * (L + 3) + 3 + (q + 1) * 512],
                    pb[grp + q][:])
            sfr.inc(ci)

    dv_g0 = sdv.n
    dv_gred, dv_gsc, dv_gcol = dv_g0 + 1, dv_g0 + 2, dv_g0 + 3
    pe_g0 = spe.n
    pe_gsum, pe_gbc = pe_g0 + 1, pe_g0 + 2
    NQ1 = 8 * 128 * 4096

    @blk.vector
    def _(v):
        v.wait_ge(sac.h, pass1_ac_end)
        v.tensor_reduce(gtmp[:, 36:37], gtmp[:, 0:16], AX, ALU.add)
        sdv.inc(v.tensor_reduce(gtmp[:, 37:38], gtmp[:, 16:32], AX, ALU.add))
        v.wait_ge(spe.h, pe_gsum)
        v.tensor_scalar(gtmp[0:1, 38:39], pb[0][0:1, 0:1],
                        0.5 / NQ1, 0.5 * EPS_Q, ALU.mult, ALU.add)
        sdv.inc(v.tensor_scalar(gtmp[0:1, 39:40], pb[1][0:1, 0:1],
                                0.5 / NQ1, 0.5 * EPS_Q, ALU.mult, ALU.add))
        v.wait_ge(spe.h, pe_gbc)
        v.tensor_copy(gcol[:, 1:2], pb[0][:, 0:1])
        v.tensor_scalar(gcol[:, 2:3], pb[0][:, 0:1], -1.0, None, ALU.mult)
        v.tensor_copy(gcol[:, 3:4], pb[1][:, 0:1])
        sdv.inc(v.tensor_scalar(gcol[:, 4:5], pb[1][:, 0:1], -1.0, None,
                                ALU.mult))

    @blk.tensor
    def _(t):
        t.wait_ge(sdv.h, dv_gred)
        t.wait_ge(sfr.h, fr0 + 15)
        t.matmul(pb[0][0:1, 0:1], ones_c[:], gtmp[:, 36:37], start=True, stop=True)
        mm = t.matmul(pb[1][0:1, 0:1], ones_c[:], gtmp[:, 37:38], start=True, stop=True)
        spe.inc(mm)
        t.wait_ge(sdv.h, dv_gsc)
        t.matmul(pb[0][:, 0:1], ones_r[:], gtmp[0:1, 38:39], start=True, stop=True)
        mm = t.matmul(pb[1][:, 0:1], ones_r[:], gtmp[0:1, 39:40], start=True, stop=True)
        spe.inc(mm)

    @blk.gpsimd
    def _(g):
        g.wait_ge(spe.h, pe_p2_h0)
        for k in range(NB_DM):
            lld.inc(g.dma_start(wbA[:, k * 2048:(k + 1) * 2048],
                                d_win[k * 128:(k + 1) * 128, 2048:4096]), 16)
    p2_w1 = lld.n

    @blk.tensor
    def _(t):
        t.wait_ge(lld.h, p2_w1)
        t.wait_ge(sdv.h, dv_gcol)
        for i in range(NB_DI):
            f = NB_DI + i
            grp = (f % 2) * 4
            if f <= 17:
                t.wait_ge(sfr.h, fr0 + f - 1)
            else:
                t.wait_ge(sac.h, ac_p2 + f - 17)
            for q in range(4):
                for k in range(NB_DM):
                    mm = t.matmul(pb[grp + q][:],
                             wbA[:, k * 2048 + i * 128:k * 2048 + i * 128 + 128],
                             xnt[:, k * L + q * 512:k * L + (q + 1) * 512],
                             start=(k == 0), stop=(k == NB_DM - 1))
            spe.inc(mm)
    pe_p2_h1 = spe.n

    @blk.scalar
    def _(sc):
        for i in range(NB_DI):
            sc.wait_ge(spe.h, pe_p2_h0 + 2 + i + 1)
            if i >= 2:
                zsp.wait(sc, i - 2)
            grp = ((NB_DI + i) % 2) * 4
            for q in range(4):
                ai = sc.activation(stgZ[:, (i % 2) * L + q * 512:(i % 2) * L + (q + 1) * 512],
                                   pb[grp + q][:], AF.Silu)
            sac.inc(ai)

    @blk.sync
    def _(s):
        for i in range(NB_DI):
            s.wait_ge(sac.h, ac_p2 + i + 1)
            zsp.issue(s.dma_start(s_z[i], stgZ[:, (i % 2) * L:(i % 2) * L + L]), i)

    p2_fr_end = sfr.n

    conv64 = a16(65632, NB_DI * L)
    cvd = a16(131232, 2 * 512)

    dv_cvd = sdv.n
    pe_p3 = spe.n
    fr_p3 = sfr.n

    @blk.vector
    def _(v):
        v.wait_ge(spe.h, pe_p2_h1)
        v.wait_ge(sac.h, pass1_ac_end)
        for j in range(NB_DI):
            if j >= 2:
                v.wait_ge(spe.h, pe_p3 + j - 1)
                jj = j - 2
                if jj == 11:
                    zsp.wait(v, 14)
                    zsp.wait(v, 15)
                grp = (jj % 2) * 4
                for q in range(4):
                    ci = v.tensor_copy(conv64[:, jj * L + q * 512:jj * L + (q + 1) * 512],
                                       pb[grp + q][:])
                sfr.inc(ci)
            for k in range(4):
                ci = v.tensor_scalar(cvd[:, (j % 2) * 512 + k * 128:(j % 2) * 512 + (k + 1) * 128],
                                     eyem[:], t_dwt[:, j * 4 + k:j * 4 + k + 1], None, ALU.mult)
            sdv.inc(ci)
        for jj in (NB_DI - 2, NB_DI - 1):
            v.wait_ge(spe.h, pe_p3 + jj + 1)
            grp = (jj % 2) * 4
            for q in range(4):
                ci = v.tensor_copy(conv64[:, jj * L + q * 512:jj * L + (q + 1) * 512],
                                   pb[grp + q][:])
            sfr.inc(ci)

    @blk.tensor
    def _(t):
        t.wait_ge(sfr.h, p2_fr_end)
        t.wait_ge(sac.h, ac_p2 + NB_DI)
        for j in range(NB_DI):
            grp = (j % 2) * 4
            t.wait_ge(sdv.h, dv_cvd + j + 1)
            if j >= 2:
                t.wait_ge(sfr.h, fr_p3 + j - 1)
            for q in range(4):
                for k in range(4):
                    mm = t.matmul(pb[grp + q][:],
                             cvd[:, (j % 2) * 512 + k * 128:(j % 2) * 512 + (k + 1) * 128],
                             xin[:, j * (L + 3) + k + q * 512:j * (L + 3) + k + q * 512 + 512],
                             start=(k == 0), stop=(k == 3))
            spe.inc(mm)

    p3_fr_end = sfr.n
    p3_dv_end = sdv.n

    pw64 = a16(0, NB_DI * DI)
    xcs = a16(131232, 2 * L)

    @blk.gpsimd
    def _(g):
        g.wait_ge(spe.h, pe_p2_h1)
        for k in range(NB_DI):
            g.wait_ge(spe.h, pe_p3 + k + 1)
            lld.inc(g.dma_start(pw64[:, k * 2048:(k + 1) * 2048],
                                d_pw[k * 128:(k + 1) * 128, :]), 16)
    p4_w = lld.n

    pe_p4 = spe.n
    ac_p4 = sac.n
    xsp = RB.start()

    @blk.tensor
    def _(t):
        t.wait_ge(lld.h, p4_w)
        t.wait_ge(sfr.h, p3_fr_end)
        for f in range(NB_DI):
            grp = (f % 2) * 4
            if f >= 2:
                t.wait_ge(sac.h, ac_p4 + f - 1)
            for q in range(4):
                for k in range(NB_DI):
                    mm = t.matmul(pb[grp + q][:],
                             pw64[:, k * 2048 + f * 128:k * 2048 + f * 128 + 128],
                             conv64[:, k * L + q * 512:k * L + (q + 1) * 512],
                             start=(k == 0), stop=(k == NB_DI - 1))
            spe.inc(mm)

    @blk.scalar
    def _(sc):
        for f in range(NB_DI):
            sc.wait_ge(spe.h, pe_p4 + f + 1)
            if f >= 2:
                xsp.wait(sc, f - 2)
            for q in range(4):
                ai = sc.activation(xcs[:, (f % 2) * L + q * 512:(f % 2) * L + (q + 1) * 512],
                                   pb[(f % 2) * 4 + q][:], AF.Silu, bias=cbias[:, f:f + 1])
            sac.inc(ai)

    @blk.sync
    def _(s):
        for f in range(NB_DI):
            s.wait_ge(sac.h, ac_p4 + f + 1)
            xsp.issue(s.dma_start(s_xc[f], xcs[:, (f % 2) * L:(f % 2) * L + L]), f)

    ac_p4_end = sac.n

    qin2 = [a32(139424 + s * 4096, 1024) for s in range(3)]
    qout2 = [a16(151712 + s * 2048, 1024) for s in range(2)]
    NQC = 64

    dv_q2b = sdv.n
    rs2 = RS3.start()
    spq = RT.start()

    def q1done(c):
        return dv_q2b + c + 1

    def q2src(c):
        if c < 32:
            return d_w1t[(c // 4) * 128:(c // 4 + 1) * 128,
                         (c % 4) * 1024:(c % 4 + 1) * 1024]
        return d_w2t[(c - 32) * 128:(c - 31) * 128, :]

    def q2dst(c):
        if c < 32:
            return s_wq1[c // 4][:, (c % 4) * 1024:(c % 4 + 1) * 1024]
        return s_wq2[c - 32]

    @blk.vector
    def _(v):
        for c in range(NQC):
            thr, nthr = (1, 2) if c < 32 else (3, 4)
            rs2.wait(v, c)
            if c >= 2:
                spq.wait(v, c - 2)
            v.tensor_scalar(qout2[c % 2][:], qin2[c % 3][:],
                            gcol[:, nthr:nthr + 1], -1.0, ALU.is_gt, ALU.add)
            sdv.inc(v.scalar_tensor_tensor(qout2[c % 2][:], qin2[c % 3][:],
                                           gcol[:, thr:thr + 1], qout2[c % 2][:],
                                           ALU.is_ge, ALU.add))

    @blk.gpsimd
    def _(g):
        def spill(c):
            spq.issue(g.dma_start(q2dst(c), qout2[c % 2][:]), c)

        g.wait_ge(sac.h, pass1_ac_end)
        for c in range(NQC):
            if c >= 3:
                g.wait_ge(sdv.h, q1done(c - 3))
                spill(c - 3)
            rs2.issue(g.dma_start(qin2[c % 3][:], q2src(c)), c)
        for c in (NQC - 3, NQC - 2, NQC - 1):
            g.wait_ge(sdv.h, q1done(c))
            spill(c)
        for k in range(NB_DI):
            lcast.inc(g.dma_start(s_wo[k * 128:(k + 1) * 128, :],
                                  d_wout[k * 128:(k + 1) * 128, :]), 16)

    xcr = a16(106496, 2 * L)
    d016 = a16(8192, L)[0:64, :]
    bc16 = a16(12288, L)[0:16, :]
    p5t = a32(16384, L)
    stgD = a16(24576, 2 * L)

    pe_p5 = spe.n
    xcl = RA.start()

    @blk.sync
    def _(s):
        for k in range(NB_DI):
            xsp.wait(s, k)
            if k >= 2:
                s.wait_ge(spe.h, pe_p5 + k - 1)
            xcl.issue(s.dma_start(xcr[:, (k % 2) * L:(k % 2) * L + L], s_xc[k]), k)

    @blk.tensor
    def _(t):
        t.wait_ge(sac.h, ac_p4 + 15)
        for k in range(NB_DI):
            xcl.wait(t, k)
            for q in range(4):
                mm = t.matmul(pb[q][0:80, :],
                         wx16[:, k * 80:(k + 1) * 80],
                         xcr[:, (k % 2) * L + q * 512:(k % 2) * L + (q + 1) * 512],
                         start=(k == 0), stop=(k == NB_DI - 1))
            spe.inc(mm)
    pe_p5a = spe.n

    @blk.vector
    def _(v):
        v.wait_ge(spe.h, pe_p5a)
        for q in range(4):
            v.tensor_copy(d016[:, q * 512:(q + 1) * 512], pb[q][0:64, :])
            ci = v.tensor_copy(bc16[:, q * 512:(q + 1) * 512], pb[q][64:80, :])
        sdv.inc(ci)

    dv_p5bc = sdv.n
    pe_p5b = spe.n
    ac_p5b = sac.n
    dsp = RB.start()

    @blk.tensor
    def _(t):
        t.wait_ge(sdv.h, dv_p5bc)
        for f in range(NB_DI):
            grp = (f % 2) * 4
            if f >= 2:
                t.wait_ge(sac.h, ac_p5b + 2 * (f - 1))
            for q in range(4):
                mm = t.matmul(pb[grp + q][:],
                         wdt16[:, f * 128:(f + 1) * 128],
                         d016[:, q * 512:(q + 1) * 512],
                         start=True, stop=True)
            spe.inc(mm)
    pe_p5b_end = spe.n

    @blk.scalar
    def _(sc):
        RB.barrier(sc)
        for f in range(NB_DI):
            sc.wait_ge(spe.h, pe_p5b + f + 1)
            if f >= 2:
                dsp.wait(sc, f - 2)
            grp = (f % 2) * 4
            for q in range(4):
                ai = sc.activation(p5t[:, q * 512:(q + 1) * 512], pb[grp + q][:],
                                   AF.Exp, bias=bdt[:, f:f + 1])
            sac.inc(ai)
            for q in range(4):
                ai = sc.activation(stgD[:, (f % 2) * L + q * 512:(f % 2) * L + (q + 1) * 512],
                                   p5t[:, q * 512:(q + 1) * 512], AF.Ln, bias=1.0)
            sac.inc(ai)

    ac_p5_end = sac.n

    @blk.sync
    def _(s):
        for f in range(NB_DI):
            s.wait_ge(sac.h, ac_p5b + 2 * (f + 1))
            dsp.issue(s.dma_start(s_dl[f], stgD[:, (f % 2) * L:(f % 2) * L + L]), f)

    brep = a16(40960, DS * L)
    crep = a16(73728, DS * L)

    pe_p5c = spe.n
    dv_rep0 = sdv.n

    @blk.tensor
    def _(t):
        t.wait_ge(sac.h, ac_p5_end)
        for n in range(2 * DS):
            grp = (n % 2) * 4
            if n >= 2:
                t.wait_ge(sdv.h, dv_rep0 + n - 1)
            for q in range(4):
                mm = t.matmul(pb[grp + q][:], ind16[:, n * 128:(n + 1) * 128],
                         bc16[:, q * 512:(q + 1) * 512], start=True, stop=True)
            spe.inc(mm)
    pe_p5c_end = spe.n

    @blk.vector
    def _(v):
        for n in range(2 * DS):
            v.wait_ge(spe.h, pe_p5c + n + 1)
            grp = (n % 2) * 4
            dst = brep if n < DS else crep
            nn = n % DS
            for q in range(4):
                ci = v.tensor_copy(dst[:, nn * L + q * 512:nn * L + (q + 1) * 512],
                                   pb[grp + q][:])
            sdv.inc(ci)

    dv_p5_end = sdv.n

    dl_s = [a16(0, L), a16(4096, L)]
    xc_s = [a16(8192, L), a16(12288, L)]
    zj_s = [a16(16384, L), a16(20480, L)]
    a_s = [a16(24576, L), a16(28672, L)]
    uj_s = a16(32768, L)
    b_s = a16(36864, L)
    h_s = a16(106496, L)
    xcd_s = a16(110592, L)
    ty2_s = a16(114688, L)
    hc_s = [a16(118784 + n * 4096, L) for n in range(DS)]
    wq1r = [a16(151552, 1024), a16(153600, 1024)]
    h1stg = a16(155808, L)

    ac_p6 = sac.n
    dv_p6 = sdv.n
    pe_p6 = spe.n
    dl6 = RA.start()
    xc6 = RC.start()
    z6 = RD.start()
    ysp = RE.start()
    rg = RG.start()
    h8 = RB.start()

    def dv_scan(i):
        j, n = divmod(i, DS)
        return dv_p6 + 31 * j + 3 * n + 3

    def dv_hc(i):
        j, n = divmod(i, DS)
        return dv_p6 + 31 * j + 3 * n + 4

    def dv_t1(j):
        return dv_p6 + 31 * j + 30

    def dv_y2(j):
        return dv_p6 + 31 * (j + 1)

    def pe_sumb(j):
        return pe_p6 + 3 * j + 1

    def ffn1_done(f):
        return pe_p6 + 3 * (f // 2) + 2 + (f % 2)

    sac_a = {}
    sac_relu = [None] * NB_FF

    @blk.scalar
    def _(sc):
        def emit_relu(f):
            sc.wait_ge(spe.h, ffn1_done(f))
            if f >= 1:
                h8.wait(sc, f - 1)
            for q in range(4):
                ai = sc.activation(h1stg[:, q * 512:(q + 1) * 512],
                                   pb[4 + q][:], AF.Relu, bias=b1t[:, f:f + 1])
            sac_relu[f] = sac.inc(ai)

        sc.wait_ge(sdv.h, dv_p5_end)
        h8.barrier_base(sc)
        for j in range(NB_DI):
            dl6.wait(sc, j)
            for n in range(DS):
                i = j * DS + n
                if i >= 2:
                    sc.wait_ge(sdv.h, dv_scan(i - 2))
                sac_a[i] = sac.inc(sc.activation(
                    a_s[i % 2], dl_s[j % 2], AF.Exp,
                    scale=aneg[:, j * DS + n:j * DS + n + 1]))
                if n == 1 and j >= 1:
                    emit_relu(2 * j - 2)
                if n == 4 and j >= 1:
                    emit_relu(2 * j - 1)
        emit_relu(2 * NB_DI - 2)
        emit_relu(2 * NB_DI - 1)

    @blk.sync
    def _(s):
        def loads(j):
            dsp.wait(s, j)
            dl6.issue(s.dma_start(dl_s[j % 2], s_dl[j]), j)
            xc6.issue(s.dma_start(xc_s[j % 2], s_xc[j]), j)
            z6.issue(s.dma_start(zj_s[j % 2], s_z[j]), j)

        s.wait_ge(spe.h, pe_p5c_end)
        spq.wait_all(s)
        loads(0)
        loads(1)
        for j in range(NB_DI):
            if j >= 1:
                s.wait_ge(sdv.h, dv_y2(j - 1))
                ysp.issue(s.dma_start(s_y2[j - 1].rearrange("t p c -> p t c"),
                                      ty2_s.rearrange("p (t c) -> p t c", t=NT)), j - 1)
                if j + 1 < NB_DI:
                    loads(j + 1)
            for f in (2 * j - 2, 2 * j - 1):
                if 0 <= f < NB_FF:
                    s.wait_ge(sac.h, sac_relu[f])
                    h8.issue(s.dma_start(
                        s_h1[f].rearrange("t p c -> p t c"),
                        h1stg[:].rearrange("p (t c) -> p t c", t=NT)), f)
            for f in (2 * j, 2 * j + 1):
                if f >= 2:
                    s.wait_ge(spe.h, ffn1_done(f - 2))
                rg.issue(s.dma_start(
                    wq1r[f % 2].rearrange("p (k c) -> p k c", k=NB_DM),
                    s_wq1[:, :, f * 128:(f + 1) * 128].rearrange("k p c -> p k c")), f)
        for f in (NB_FF - 2, NB_FF - 1):
            s.wait_ge(sac.h, sac_relu[f])
            h8.issue(s.dma_start(
                s_h1[f].rearrange("t p c -> p t c"),
                h1stg[:].rearrange("p (t c) -> p t c", t=NT)), f)
        s.wait_ge(sdv.h, dv_y2(NB_DI - 1))
        ysp.issue(s.dma_start(s_y2[NB_DI - 1].rearrange("t p c -> p t c"),
                              ty2_s.rearrange("p (t c) -> p t c", t=NT)), NB_DI - 1)

    @blk.vector
    def _(v):
        h8.barrier_base(v)
        spq.wait_all(v)
        for j in range(NB_DI):
            dl6.wait(v, j)
            xc6.wait(v, j)
            if j >= 1:
                v.wait_ge(spe.h, pe_sumb(j - 1))
            sdv.inc(v.tensor_tensor(uj_s, dl_s[j % 2], xc_s[j % 2], ALU.mult))
            for n in range(DS):
                i = j * DS + n
                v.wait_ge(sac.h, sac_a[i])
                sdv.inc(v.tensor_tensor(b_s, uj_s, brep[:, n * L:(n + 1) * L], ALU.mult))
                sdv.inc(v.tensor_tensor_scan(h_s, a_s[i % 2], b_s, 0.0, ALU.mult, ALU.add))
                sdv.inc(v.tensor_tensor(hc_s[n], h_s, crep[:, n * L:(n + 1) * L], ALU.mult))
            xc6.wait(v, j)
            sdv.inc(v.tensor_scalar(xcd_s, xc_s[j % 2], dtab[:, j:j + 1], None, ALU.mult))
            v.wait_ge(spe.h, pe_sumb(j))
            if j >= 1:
                ysp.wait(v, j - 1)
            for q in range(4):
                sdv.inc(v.tensor_tensor(ty2_s[:, q * 512:(q + 1) * 512], pb[q][:],
                                        xcd_s[:, q * 512:(q + 1) * 512], ALU.add))
            z6.wait(v, j)
            sdv.inc(v.tensor_tensor(ty2_s, ty2_s, zj_s[j % 2], ALU.mult))
    assert sdv.n == dv_p6 + 31 * NB_DI

    @blk.tensor
    def _(t):
        for j in range(NB_DI):
            if j >= 1:
                t.wait_ge(sdv.h, dv_t1(j - 1))
            for n in range(DS):
                t.wait_ge(sdv.h, dv_hc(j * DS + n))
                for q in range(4):
                    mm = t.matmul(pb[q][:], eye16[:],
                                  hc_s[n][:, q * 512:(q + 1) * 512],
                                  start=(n == 0), stop=(n == DS - 1))
            spe.inc(mm)
            for f in (2 * j, 2 * j + 1):
                rg.wait(t, f)
                if f >= 1:
                    t.wait_ge(sac.h, sac_relu[f - 1])
                for q4 in range(4):
                    for k in range(NB_DM):
                        mm = t.matmul(pb[4 + q4][:],
                                      wq1r[f % 2][:, k * 128:(k + 1) * 128],
                                      xnt[:, k * L + q4 * 512:k * L + (q4 + 1) * 512],
                                      start=(k == 0), stop=(k == NB_DM - 1))
                spe.inc(mm)
    pe_p6_end = spe.n
    assert pe_p6_end == pe_p6 + 3 * NB_DI

    p6_dv_end = sdv.n

    wo16 = a16(0, NB_DI * DM)
    wq2f = a16(32768, NB_FF * DM)
    lr6 = [a16(98304 + s * 1024, 512) for s in range(6)]
    xre2 = [a32(110592 + s * 8192, 2 * DM) for s in range(2)]
    outst2 = [a32(126976 + s * 8192, 2 * DM) for s in range(2)]
    b2bc = a32(143360, DM)

    NKK = NB_DI + NB_FF
    pe_pt0 = spe.n
    dv_pt0 = sdv.n
    dv_b2bc = dv_pt0 + 1

    def pe_kk(i):
        return pe_pt0 + 2 + i

    def dv_evac(g8):
        return dv_pt0 + 2 + g8

    def lhs_consumed(c):
        g, qq = c // 24, (c % 24) // 2
        return pe_kk(g * NKK + 4 * qq + 3)

    wring = RW.start()
    lr = RL.start()
    xrl = RC.start()
    org = RD.start()

    @blk.sync
    def _(s):
        s.wait_ge(sdv.h, p6_dv_end)
        s.wait_ge(spe.h, pe_p6_end)
        s.wait_ge(lcast.h, 16 * NB_DI)
        h8.wait_all(s)
        ysp.wait_all(s)
        for g8 in range(8):
            if g8 >= 2:
                s.wait_ge(sdv.h, dv_evac(g8 - 2))
            xrl.issue(s.dma_start(
                xre2[g8 % 2].rearrange("p (t d) -> p t d", t=2),
                d_x.rearrange("(t p) d -> p t d", p=128)[:, 2 * g8:2 * g8 + 2, :]), g8)
            for q4 in range(NKK // 4):
                q = g8 * (NKK // 4) + q4
                if g8 == 0:
                    for kk in range(4 * q4, 4 * q4 + 4):
                        if kk >= 12:
                            s.wait_ge(spe.h, pe_kk(kk - 12))
                        if kk < NB_DI:
                            wring.issue(s.dma_start(
                                wo16[:, kk * 1024:(kk + 1) * 1024],
                                s_wo[kk * 128:(kk + 1) * 128, :]), kk)
                        else:
                            wring.issue(s.dma_start(
                                wq2f[:, (kk - NB_DI) * 1024:(kk - NB_DI + 1) * 1024],
                                s_wq2[kk - NB_DI]), kk)
                j0 = 4 * q4
                srcb = (s_y2[j0:j0 + 4] if j0 < NB_DI else s_h1[j0 - NB_DI:j0 - NB_DI + 4])
                for tt in range(2):
                    c = q * 2 + tt
                    if c >= 6:
                        s.wait_ge(spe.h, lhs_consumed(c - 6))
                    lr.issue(s.dma_start(
                        lr6[c % 6].rearrange("p (j c) -> p j c", j=4),
                        srcb[:, 2 * g8 + tt].rearrange("j p c -> p j c")), c)
            if g8 >= 1:
                s.wait_ge(sdv.h, dv_evac(g8 - 1))
                org.issue(s.dma_start(
                    d_out.rearrange("(t p) d -> p t d", p=128)[:, 2 * (g8 - 1):2 * g8, :],
                    outst2[(g8 - 1) % 2].rearrange("p (t d) -> p t d", t=2)), g8 - 1)
        s.wait_ge(sdv.h, dv_evac(7))
        org.issue(s.dma_start(
            d_out.rearrange("(t p) d -> p t d", p=128)[:, 14:16, :],
            outst2[1].rearrange("p (t d) -> p t d", t=2)), 7)
        org.wait_all(s)

    @blk.tensor
    def _(t):
        t.wait_ge(sld.h, p0_sld)
        t.wait_ge(sac.h, sac_relu[NB_FF - 1])
        mm = t.matmul(pb[4][:], ones_r[:], b2row[:, 0:512], start=True, stop=True)
        mm = t.matmul(pb[5][:], ones_r[:], b2row[:, 512:1024], start=True, stop=True)
        spe.inc(mm)
        t.wait_ge(sdv.h, p6_dv_end)
        for g8 in range(8):
            B = (g8 % 2) * 4
            if g8 == 1:
                t.wait_ge(sdv.h, dv_b2bc)
            if g8 >= 2:
                t.wait_ge(sdv.h, dv_evac(g8 - 2))
            for kk in range(NKK):
                q = (g8 * NKK + kk) // 4
                if g8 == 0:
                    wring.wait(t, kk)
                if kk % 4 == 0:
                    lr.wait(t, 2 * q)
                    lr.wait(t, 2 * q + 1)
                w = wo16 if kk < NB_DI else wq2f
                co = (kk if kk < NB_DI else kk - NB_DI) * 1024
                lo = (kk % 4) * 128
                for tt in range(2):
                    for h in range(2):
                        mm = t.matmul(pb[B + tt * 2 + h][:],
                                      lr6[(2 * q + tt) % 6][:, lo:lo + 128],
                                      w[:, co + h * 512:co + (h + 1) * 512],
                                      start=(kk == 0), stop=(kk == NKK - 1))
                spe.inc(mm)

    @blk.vector
    def _(v):
        v.wait_ge(spe.h, pe_pt0 + 1)
        v.tensor_copy(b2bc[:, 0:512], pb[4][:])
        ci = v.tensor_copy(b2bc[:, 512:1024], pb[5][:])
        sdv.inc(ci)
        for g8 in range(8):
            B = (g8 % 2) * 4
            v.wait_ge(spe.h, pe_kk(g8 * NKK + NKK - 1))
            xrl.wait(v, g8)
            if g8 >= 2:
                org.wait(v, g8 - 2)
            xs = xre2[g8 % 2]
            v.tensor_tensor(xs[:, 0:1024], xs[:, 0:1024], b2bc[:], ALU.add)
            v.tensor_tensor(xs[:, 1024:2048], xs[:, 1024:2048], b2bc[:], ALU.add)
            for tt in range(2):
                for h in range(2):
                    c0 = tt * 1024 + h * 512
                    ci = v.tensor_tensor(outst2[g8 % 2][:, c0:c0 + 512],
                                         pb[B + tt * 2 + h][:],
                                         xs[:, c0:c0 + 512], ALU.add)
            sdv.inc(ci)

    if dbg:
        d_dbg = {}
        for nm, st_t, shp in [("dbg_z", s_z, [NB_DI, 128, L]),
                              ("dbg_xc", s_xc, [NB_DI, 128, L]),
                              ("dbg_dl", s_dl, [NB_DI, 128, L]),
                              ("dbg_y2", s_y2, [NB_DI, NT, 128, 128]),
                              ("dbg_wq1", s_wq1, [NB_DM, 128, DFF]),
                              ("dbg_h1", s_h1, [NB_FF, NT, 128, 128])]:
            d_dbg[nm] = (nc.dram_tensor(nm, shp, BF16, kind="ExternalOutput").ap(), st_t)
        d_xnt_o = nc.dram_tensor("dbg_xnt", [128, NB_DM * L], BF16, kind="ExternalOutput").ap()
        d_gcol_o = nc.dram_tensor("dbg_gcol", [128, 6], F32, kind="ExternalOutput").ap()
        d_gtmp_o = nc.dram_tensor("dbg_gtmp", [128, 40], F32, kind="ExternalOutput").ap()

        @blk.sync
        def _(s):
            org.wait_all(s)
            for nm, (dst, st_t) in d_dbg.items():
                sld.inc(s.dma_start(dst, st_t), 16)
            sld.inc(s.dma_start(d_xnt_o, xnt[:]), 16)
            sld.inc(s.dma_start(d_gcol_o, gcol[:]), 16)
            sld.inc(s.dma_start(d_gtmp_o, gtmp[:]), 16)
            s.wait_ge(sld.h, sld.n)

    blk.__exit__(None, None, None)
    for cm in reversed(list(live.values())):
        cm.__exit__(None, None, None)
    nc.finalize()
    return nc


def kernel(**inputs):
    xfull = np.ascontiguousarray(inputs["L0_attn_x"], dtype=np.float32)
    w_in = np.ascontiguousarray(inputs["L0_attn_W_in"], dtype=np.float32)
    pw = np.ascontiguousarray(inputs["L0_attn_pw_kernel"], dtype=np.float32)
    w_x = np.ascontiguousarray(
        np.asarray(inputs["L0_attn_W_x"], dtype=np.float32)
        .reshape(NB_DI, 128, 80).transpose(1, 0, 2).reshape(128, NB_DI * 80))
    w_dt = np.ascontiguousarray(inputs["L0_attn_W_dt"], dtype=np.float32)
    w_out = np.ascontiguousarray(inputs["L0_attn_W_out"], dtype=np.float32)
    w1t = np.ascontiguousarray(np.asarray(inputs["L0_ffn_W1"], dtype=np.float32).T)
    w2t = np.ascontiguousarray(np.asarray(inputs["L0_ffn_W2"], dtype=np.float32).T)
    def blocky(vec, nb):
        return np.ascontiguousarray(
            np.asarray(vec, dtype=np.float32).reshape(nb, 128).T)

    dwt = np.ascontiguousarray(
        np.asarray(inputs["L0_attn_dw_kernel"], dtype=np.float32).T
        .reshape(NB_DI, 128, 4).transpose(1, 0, 2).reshape(128, NB_DI * 4))
    alog = np.ascontiguousarray(
        np.asarray(inputs["L0_attn_A_log"], dtype=np.float32)
        .reshape(NB_DI, 128, DS).transpose(1, 0, 2).reshape(128, NB_DI * DS))
    dvec = blocky(inputs["L0_attn_D"], NB_DI)
    cb = blocky(inputs["L0_attn_conv_bias"], NB_DI)
    bdtv = blocky(inputs["L0_attn_b_dt"], NB_DI)
    b1 = blocky(inputs["L0_ffn_b1"], NB_FF)
    b2 = np.ascontiguousarray(np.asarray(inputs["L0_ffn_b2"], dtype=np.float32).reshape(1, DM))

    eye16 = np.eye(128, dtype=ml_dtypes.bfloat16)
    eyem = np.eye(128, dtype=np.float32)
    ind = np.zeros((16, 16 * 128), dtype=ml_dtypes.bfloat16)
    for ssel in range(16):
        ind[ssel, ssel * 128:(ssel + 1) * 128] = 1.0
    ones_r = np.ones((1, 128), dtype=np.float32)
    ones_c = np.ones((128, 1), dtype=np.float32)

    nc = build_core()
    shared = dict(w_in=w_in, pw=pw, w_x=w_x, w_dt=w_dt, w_out=w_out, w1t=w1t,
                  w2t=w2t, dwt=dwt, a_log=alog, d_vec=dvec, conv_bias=cb,
                  b_dt=bdtv, b1=b1, b2=b2, eye16=eye16, eyem=eyem, ind=ind,
                  ones_r=ones_r, ones_c=ones_c)
    in_maps = [dict(shared, x=np.ascontiguousarray(xfull[b])) for b in range(8)]
    res = run_bass_kernel_spmd(nc, in_maps, core_ids=list(range(8)))
    return np.stack([res.results[b]["out"] for b in range(8)], axis=0)
